# version 35
# baseline (speedup 1.0000x reference)
import numpy as np
import ml_dtypes
from contextlib import ExitStack

import concourse.bass as bass
import concourse.mybir as mybir
from concourse.bass_utils import run_bass_kernel_spmd

F32 = mybir.dt.float32
BF16 = mybir.dt.bfloat16
I32 = mybir.dt.int32
BF = ml_dtypes.bfloat16
AF = mybir.ActivationFunctionType
ALU = mybir.AluOpType

FILLER_MODE = 4
DRAIN_ON = True
POPLOG = []
D = 1024
C = 8
TA = 4096
TO = 2048
NBA = 8
NBO = 4
HID = 3072
MH = 24
KC = 32
HP = 8
EPS = 1e-5
NEG_LN8 = -2.0794415416798357


class Prog:

    ENGINES = ("sync", "tensor", "vector", "scalar", "gpsimd")

    def __init__(self, nc, ctx):
        self.nc = nc
        self.ctx = ctx
        self.sems = {}
        self.producer = {}
        self.cnt = {}
        self.lastwait = {}
        self.q = {e: [] for e in self.ENGINES}

    def _sem(self, name):
        if name not in self.sems:
            self.sems[name] = self.ctx.enter_context(self.nc.semaphore(name))
            self.cnt[name] = 0
        return self.sems[name]

    def op(self, eng, fn):
        self.q[eng].append(fn)

    def inc(self, eng, sem, fn, n=1):
        s = self._sem(sem)
        self.producer.setdefault(sem, eng)
        self.cnt[sem] += n
        self.q[eng].append(lambda E, fn=fn, s=s, n=n: fn(E).then_inc(s, n))
        return self.cnt[sem]

    def wait(self, eng, sem, val=None):
        self._sem(sem)
        val = self.cnt[sem] if val is None else val
        assert val <= self.cnt[sem], (eng, sem, val, self.cnt[sem])
        key = (eng, sem)
        if self.lastwait.get(key, 0) >= val:
            return
        self.lastwait[key] = val
        s = self.sems[sem]
        self.q[eng].append(lambda E, s=s, v=val: E.wait_ge(s, v))

    def barrier(self):
        for eng in self.ENGINES:
            for sem in list(self.sems):
                if self.producer.get(sem, None) != eng:
                    self.wait(eng, sem)

    def dma_done(self, eng):
        for sem in list(self.sems):
            if self.producer.get(sem) in ("sync", "gpsimd") and                     self.producer.get(sem) != eng:
                self.wait(eng, sem)

    def flush(self, block):
        for eng in self.ENGINES:
            ops = self.q[eng]
            self.q[eng] = []
            if not ops:
                continue
            getattr(block, eng)(lambda E, ops=ops: [f(E) for f in ops])


def build_nc(dump=False):
    nc = bass.Bass("TRN2", target_bir_lowering=False, detect_race_conditions=False)
    _orig_mm = nc.tensor.matmul

    def _mm(*a, **k):
        k.setdefault("skip_group_check", True)
        return _orig_mm(*a, **k)
    try:
        nc.tensor.matmul = _mm
        mm_patched = True
    except (AttributeError, TypeError):
        mm_patched = False
    scratch_kind = "ExternalOutput" if dump else "Internal"

    xt = nc.dram_tensor("xt", [D, TA], F32, kind="ExternalInput").ap()
    wqt = nc.dram_tensor("wqt", [D, D], F32, kind="ExternalInput").ap()
    wkt = nc.dram_tensor("wkt", [D, D], F32, kind="ExternalInput").ap()
    wvt = nc.dram_tensor("wvt", [D, D], F32, kind="ExternalInput").ap()
    wot = nc.dram_tensor("wot", [D, D], F32, kind="ExternalInput").ap()
    bq = nc.dram_tensor("bq", [D], F32, kind="ExternalInput").ap()
    bk = nc.dram_tensor("bk", [D], F32, kind="ExternalInput").ap()
    bv = nc.dram_tensor("bv", [D], F32, kind="ExternalInput").ap()
    bo = nc.dram_tensor("bo", [D], F32, kind="ExternalInput").ap()
    wgt = nc.dram_tensor("wgt", [D, HID], I32, kind="ExternalInput").ap()
    wut = nc.dram_tensor("wut", [D, HID], I32, kind="ExternalInput").ap()
    wdt = nc.dram_tensor("wdt", [HID, D], I32, kind="ExternalInput").ap()
    yt = nc.dram_tensor("yt", [D, TO], F32, kind="ExternalOutput").ap()

    dbg = nc.dram_tensor("dbg", [1, 48], F32, kind=scratch_kind).ap()

    sdt = mybir.dt.uint16 if dump else BF16

    def _scr(name, shape):
        ap = nc.dram_tensor(name, shape, sdt, kind=scratch_kind).ap()
        return ap.bitcast(BF16) if dump else ap
    k_scr = _scr("k_scr", [HP, 128, TA])
    q_scr = _scr("q_scr", [HP, 128, TO])
    v_scr = _scr("v_scr", [KC, 128, D])
    h_scr = _scr("h_scr", [MH, 128, TO])
    wd_scr = _scr("wd_scr", [MH, 128, D])

    xt3 = xt.rearrange("(c p) t -> p c t", p=128)
    yt3 = yt.rearrange("(m p) t -> p m t", p=128)

    with ExitStack() as g:
        E = g.enter_context
        P = Prog(nc, g)

        psumS = E(nc.psum_tensor("psumS", [128, 2048], F32))
        psB2 = E(nc.psum_tensor("psB2", [128, 2048], F32))
        psB = psB2[:, 0:1024]
        ctx_ps = psB2[:, 1024:1536]
        den_ps = psB2[:, 1536:2048]
        BK = [psumS[:, i * 512:(i + 1) * 512] for i in range(4)] + \
             [psB2[:, i * 512:(i + 1) * 512] for i in range(4)]

        consts = E(nc.sbuf_tensor("consts", [1, 48], F32))
        ones_f1 = E(nc.sbuf_tensor("ones_f1", [1, 128], F32))
        ones_bf = E(nc.sbuf_tensor("ones_bf", [128, 1], BF16))
        ones1_bf = E(nc.sbuf_tensor("ones1_bf", [1, 128], BF16))
        negc_sb = E(nc.sbuf_tensor("negc_sb", [128, HP], F32))
        bqb = E(nc.sbuf_tensor("bqb", [128, C], F32))
        bkb = E(nc.sbuf_tensor("bkb", [128, C], F32))
        bob = E(nc.sbuf_tensor("bob", [128, C], F32))
        r_mean = E(nc.sbuf_tensor("r_mean", [1, 512], F32))
        r_msq = E(nc.sbuf_tensor("r_msq", [1, 512], F32))
        r_var = E(nc.sbuf_tensor("r_var", [1, 512], F32))
        r_rstd = E(nc.sbuf_tensor("r_rstd", [1, 512], F32))
        r_nmr = E(nc.sbuf_tensor("r_nmr", [1, 512], F32))
        r_rA = E(nc.sbuf_tensor("r_rA", [1, 512], F32))
        r_rB = E(nc.sbuf_tensor("r_rB", [1, 512], F32))

        def memsets(v):
            v.memset(ones_f1[:], 1.0)
            v.memset(ones_bf[:], 1.0)
            v.memset(ones1_bf[:], 1.0)
            v.memset(consts[0:1, 0:1], EPS)
            v.memset(consts[0:1, 1:2], 0.0)
            v.memset(consts[0:1, 2:3], NEG_LN8)
            return v.memset(consts[0:1, 16:32], 0.0)
        P.inc("vector", "ve", memsets)
        MEMSET_VE = P.cnt["ve"]

        for name, src, dst in (("bq", bq, bqb), ("bk", bk, bkb), ("bo", bo, bob)):
            P.inc("sync", "lb", lambda E_, s=src, d=dst:
                  E_.dma_start(out=d[:], in_=s.rearrange("(m p) -> p m", p=128),
                               allow_slow_non_contiguous=True), 16)

        block = E(nc.Block())

        ph1wrap = ExitStack()
        xn_own = ph1wrap.enter_context(nc.sbuf_tensor("xn_own", [128, C, TO], BF16))
        wq_bf = ph1wrap.enter_context(nc.sbuf_tensor("wq_bf", [128, C, D], BF16))
        with ExitStack() as p1a:
            E1 = p1a.enter_context
            wk_bf = E1(nc.sbuf_tensor("wk_bf", [128, C, D], BF16))
            wv_bf = E1(nc.sbuf_tensor("wv_bf", [128, C, D], BF16))
            wstage = E1(nc.sbuf_tensor("wstage", [128, 4, D], F32))
            wstageB = E1(nc.sbuf_tensor("wstageB", [128, 2, D], F32))
            x_stage = E1(nc.sbuf_tensor("x_stage", [128, 2, C, 512], F32))
            xb_bf = E1(nc.sbuf_tensor("xb_bf", [128, C, 512], BF16))
            xsq_bf = E1(nc.sbuf_tensor("xsq_bf", [128, C, 512], BF16))
            xn_blk = E1(nc.sbuf_tensor("xn_blk", [128, 2, C, 512], BF16))
            tmp_sb = E1(nc.sbuf_tensor("tmp_sb", [128, 512], F32))
            kev = E1(nc.sbuf_tensor("kev", [128, 4, 512], BF16))
            ksq = E1(nc.sbuf_tensor("ksq", [128, C, 512], BF16))
            vev = E1(nc.sbuf_tensor("vev", [128, 2, 1024], BF16))
            bv_f = E1(nc.sbuf_tensor("bv_f", [1, D], F32))
            bv_bf = E1(nc.sbuf_tensor("bv_bf", [1, D], BF16))

            bvld = P.inc("sync", "lbv", lambda E_:
                         E_.dma_start(out=bv_f[:], in_=bv.rearrange("(a t) -> a t", a=1)), 16)
            P.wait("vector", "lbv")
            P.inc("vector", "ve", lambda v: v.tensor_copy(bv_bf[:], bv_f[:]))

            st = {"vev_evac_ve": 0, "stat_rd_ve": 0, "stats_pe": 0,
                  "wcast": P.cnt["ve"]}
            xuse_ve = [0, 0]
            xnblk_pe = [0, 0]
            blk = {}
            P.inc("sync", "lx0", lambda E_:
                  E_.dma_start(out=x_stage[:, 0], in_=xt3[:, :, 0:512]), 16)

            def wcast_chunk(wsrc, wdst, gi):
                ws = st["wcnt"] = st.get("wcnt", 0)
                slot = ws % 4
                st["wcnt"] = ws + 1
                P.wait("sync", "ve", st.get("wc%d" % slot, 0))
                P.inc("sync", "lw%d" % slot, lambda E_, s=wsrc, gi=gi, slot=slot:
                      E_.dma_start(
                          out=wstage[:, slot:slot + 1, :],
                          in_=s.rearrange("(c p) f -> p c f", p=128)[:, gi:gi + 1, :]), 16)
                P.wait("vector", "lw%d" % slot)
                st["wc%d" % slot] = P.inc(
                    "vector", "ve", lambda v, d=wdst, gi=gi, slot=slot:
                    v.tensor_copy(d[:, gi:gi + 1, :], wstage[:, slot:slot + 1, :]))
                return st["wc%d" % slot]

            def wcast_chunkB(wsrc, wdst, gi):
                ws = st["wcntB"] = st.get("wcntB", 0)
                slot = ws % 2
                st["wcntB"] = ws + 1
                P.wait("gpsimd", "sc", st.get("wcB%d" % slot, 0))
                P.inc("gpsimd", "lwB%d" % slot, lambda E_, s=wsrc, gi=gi, slot=slot:
                      E_.dma_start(
                          out=wstageB[:, slot:slot + 1, :],
                          in_=s.rearrange("(c p) f -> p c f", p=128)[:, gi:gi + 1, :]), 16)
                P.wait("scalar", "lwB%d" % slot)
                st["wcB%d" % slot] = P.inc(
                    "scalar", "sc", lambda s_, d=wdst, gi=gi, slot=slot:
                    s_.activation(d[:, gi:gi + 1, :], wstageB[:, slot:slot + 1, :],
                                  AF.Copy))
                return st["wcB%d" % slot]

            def preA(b):
                xbuf = b % 2
                if b + 1 < NBA:
                    P.wait("sync", "ve", xuse_ve[(b + 1) % 2])
                    P.inc("sync", "lx%d" % ((b + 1) % 2), lambda E_, b=b:
                          E_.dma_start(out=x_stage[:, (b + 1) % 2],
                                       in_=xt3[:, :, (b + 1) * 512:(b + 2) * 512]), 16)
                P.wait("vector", "lb")
                P.wait("scalar", "lx%d" % xbuf)
                P.wait("scalar", "pe", st["stats_pe"])
                P.op("scalar", lambda s, xbuf=xbuf: s.activation(
                    xsq_bf[:].rearrange("p c t -> p (c t)"),
                    x_stage[:, xbuf].rearrange("p c t -> p (c t)"), AF.Square))
                P.inc("scalar", "sc", lambda s, xbuf=xbuf: s.activation(
                    xb_bf[:].rearrange("p c t -> p (c t)"),
                    x_stage[:, xbuf].rearrange("p c t -> p (c t)"), AF.Copy))
                st["xbf_sc"] = P.cnt["sc"]

            def preStats(b):
                P.wait("tensor", "sc", st["xbf_sc"])
                P.wait("tensor", "ve", st["stat_rd_ve"])
                for c in range(C):
                    P.op("tensor", lambda t, c=c: t.matmul(
                        psumS[32:33, 0:512], ones_bf[:], xb_bf[:, c, :],
                        start=(c == 0), stop=(c == C - 1), tile_position=(0, 32)))
                for c in range(C):
                    fn = lambda t, c=c: t.matmul(
                        psumS[64:65, 0:512], ones_bf[:], xsq_bf[:, c, :],
                        start=(c == 0), stop=(c == C - 1), tile_position=(0, 64))
                    if c == C - 1:
                        P.inc("tensor", "pe", fn)
                    else:
                        P.op("tensor", fn)
                st["stats_pe"] = P.cnt["pe"]
                P.wait("vector", "pe", st["stats_pe"])
                P.op("vector", lambda v: v.tensor_scalar_mul(r_mean[:], psumS[32:33, 0:512], 1.0 / D))
                P.op("vector", lambda v: v.tensor_scalar_mul(r_msq[:], psumS[64:65, 0:512], 1.0 / D))
                P.op("vector", lambda v: v.tensor_mul(tmp_sb[0:1, :], r_mean[:], r_mean[:]))
                P.inc("vector", "ve", lambda v: v.tensor_sub(r_var[:], r_msq[:], tmp_sb[0:1, :]))
                var_ve = P.cnt["ve"]
                st["stat_rd_ve"] = var_ve
                P.wait("scalar", "ve", var_ve)
                P.op("scalar", lambda s: s.activation(tmp_sb[0:1, :], r_var[:], AF.Ln,
                                                      bias=consts[0:1, 0:1]))
                P.inc("scalar", "sc", lambda s: s.activation(r_rstd[:], tmp_sb[0:1, :], AF.Exp,
                                                             bias=consts[0:1, 1:2], scale=-0.5))
                rstd_sc = P.cnt["sc"]
                P.wait("vector", "sc", rstd_sc)
                P.inc("vector", "ve", lambda v: v.scalar_tensor_tensor(
                    r_nmr[:], r_mean[:], -1.0, r_rstd[:], ALU.mult, ALU.mult))
                st["nmr_ve"] = P.cnt["ve"]

            def preBC(b):
                P.wait("tensor", "ve", st["nmr_ve"])
                P.op("tensor", lambda t: t.matmul(psB2[:, 0:512], ones_f1[:], r_rstd[:],
                                                  start=True, stop=True))
                P.inc("tensor", "pe", lambda t: t.matmul(psB2[:, 512:1024], ones_f1[:], r_nmr[:],
                                                         start=True, stop=True))
                st["bc_pe"] = P.cnt["pe"]

            def preApplyPiece(b, c0, c1, first, last):
                own = b < NBO
                tb = b * 512
                xbuf = b % 2
                if first:
                    P.wait("vector", "pe", st["bc_pe"])
                    if own:
                        st["xn_cur"] = xn_own[:, :, tb:tb + 512]
                    else:
                        st["xn_cur"] = xn_blk[:, b % 2]
                        P.wait("vector", "pe", xnblk_pe[b % 2])
                xn_cur = st["xn_cur"]
                for c in range(c0, c1):
                    P.op("vector", lambda v, c=c, xbuf=xbuf: v.tensor_mul(
                        tmp_sb[:], x_stage[:, xbuf, c, :], psB2[:, 0:512]))
                    P.op("vector", lambda v, c=c, xn_cur=xn_cur: v.tensor_add(
                        xn_cur[:, c, :], tmp_sb[:], psB2[:, 512:1024]))
                if last:
                    P.inc("vector", "ve", lambda v: v.engine_nop())
                    xuse_ve[xbuf] = P.cnt["ve"]
                    blk[b] = (xn_cur, P.cnt["ve"])

            def preApply(b):
                preApplyPiece(b, 0, C, True, True)

            def pre_K(b):
                preA(b)
                preStats(b)
                preBC(b)
                preApply(b)

            def KV(b, nxt=None):
                own = b < NBO
                tb = b * 512
                xn_cur, xn_ve = blk[b]
                kmm_pe_at = {}
                ksq_sc_at = {}
                kev_ve_at = {}
                if nxt is not None and b == 0:
                    preA(nxt)
                for m in range(C):
                    slot = m % 4
                    kbank = BK[1] if m % 2 == 0 else BK[6]
                    if nxt is not None:
                        if m == 2:
                            preStats(nxt)
                        elif m == 4:
                            preBC(nxt)
                            preApplyPiece(nxt, 0, 3, True, False)
                        elif m == 5:
                            preApplyPiece(nxt, 3, 6, False, False)
                        elif m == 6:
                            preApplyPiece(nxt, 6, C, False, True)
                            if nxt + 1 < NBA:
                                preA(nxt + 1)
                    P.wait("tensor", "ve", xn_ve)
                    if m == 0 and b == 0:
                        P.wait("tensor", "ve", st["wcast"])
                    if m >= 2:
                        P.wait("tensor", "ve", kev_ve_at[m - 2])
                    for c in range(C):
                        fn = lambda t, m=m, c=c, xn_cur=xn_cur, kbank=kbank: t.matmul(
                            kbank, wk_bf[:, c, m * 128:(m + 1) * 128],
                            xn_cur[:, c, :], start=(c == 0), stop=(c == C - 1))
                        if c == C - 1:
                            kmm_pe_at[m] = P.inc("tensor", "pe", fn)
                        else:
                            P.op("tensor", fn)
                    P.wait("vector", "pe", kmm_pe_at[m])
                    P.wait("vector", "sk%d" % slot)
                    P.inc("vector", "ve", lambda v, m=m, slot=slot, kbank=kbank:
                          v.tensor_scalar_add(kev[:, slot, :], kbank, bkb[:, m:m + 1]))
                    kev_ve_at[m] = P.cnt["ve"]
                    P.wait("scalar", "ve", kev_ve_at[m])
                    P.inc("scalar", "sc", lambda s, slot=slot, m=m: s.activation(
                        ksq[:, m, :], kev[:, slot, :], AF.Square))
                    ksq_sc_at[m] = P.cnt["sc"]
                    P.wait("gpsimd", "ve", kev_ve_at[m])
                    P.inc("gpsimd", "sk%d" % slot, lambda E_, m=m, tb=tb, slot=slot:
                          E_.dma_start(out=k_scr[m, :, tb:tb + 512],
                                       in_=kev[:, slot, :]), 16)
                for tt in range(4):
                    slot = tt % 2
                    if b == 0 and tt == 0:
                        P.wait("tensor", "sc", st["wvcast"])
                    P.wait("tensor", "ve", st["vev_evac_ve"])
                    for half in range(2):
                        bank = psumS[:, 1024 + half * 512: 1536 + half * 512]
                        for c in range(C):
                            P.op("tensor", lambda t, c=c, tt=tt, half=half, xn_cur=xn_cur,
                                 bank=bank: t.matmul(
                                     bank, xn_cur[:, c, tt * 128:(tt + 1) * 128],
                                     wv_bf[:, c, half * 512:(half + 1) * 512],
                                     start=(c == 0), stop=False))
                        fn = lambda t, half=half, bank=bank: t.matmul(
                            bank, ones1_bf[:], bv_bf[0:1, half * 512:(half + 1) * 512],
                            start=False, stop=True)
                        if half == 1:
                            P.inc("tensor", "pe", fn)
                        else:
                            P.op("tensor", fn)
                    vmm_pe = P.cnt["pe"]
                    P.wait("vector", "pe", vmm_pe)
                    P.wait("vector", "sv%d" % slot)
                    P.inc("vector", "ve", lambda v, slot=slot: v.tensor_copy(
                        vev[:, slot, :], psumS[:, 1024:2048]))
                    vev_ve = P.cnt["ve"]
                    st["vev_evac_ve"] = vev_ve
                    P.wait("gpsimd", "ve", vev_ve)
                    P.inc("gpsimd", "sv%d" % slot, lambda E_, b=b, tt=tt, slot=slot:
                          E_.dma_start(out=v_scr[b * 4 + tt, :, :],
                                       in_=vev[:, slot, :]), 16)
                if not own:
                    xnblk_pe[b % 2] = P.cnt["pe"]
                for half in range(2):
                    P.wait("tensor", "sc", ksq_sc_at[half * 4 + 3])
                    P.wait("tensor", "ve", st.get("kn_red_ve", 0))
                    for i in range(4):
                        m = half * 4 + i
                        fn = lambda t, m=m, i=i: t.matmul(
                            psB2[32 * i:32 * i + 1, 1536:2048], ones_bf[:], ksq[:, m, :],
                            start=True, stop=True, tile_position=(0, 32 * i))
                        if i == 3:
                            P.inc("tensor", "pe", fn)
                        else:
                            P.op("tensor", fn)
                    kn_pe = P.cnt["pe"]
                    P.wait("vector", "pe", kn_pe)
                    for i in range(4):
                        m = half * 4 + i
                        P.op("vector", lambda v, i=i: v.reduce_max(
                            consts[0:1, 10:11], psB2[32 * i:32 * i + 1, 1536:2048],
                            axis=mybir.AxisListType.X))
                        P.inc("vector", "ve", lambda v, m=m: v.tensor_max(
                            consts[0:1, 24 + m:25 + m], consts[0:1, 24 + m:25 + m],
                            consts[0:1, 10:11]))
                    st["kn_red_ve"] = P.cnt["ve"]

            pre_K(0)
            for gidx in range(C):
                st["wcast"] = wcast_chunk(wkt, wk_bf, gidx)
                st["wvcast"] = wcast_chunkB(wvt, wv_bf, gidx)
            WQ_VE = 0
            for b in range(NBA):
                KV(b, nxt=(b + 1 if b + 1 < NBA else None))
                if b >= 4:
                    for gidx in (2 * (b - 4), 2 * (b - 4) + 1):
                        WQ_VE = wcast_chunk(wqt, wq_bf, gidx)
            P.flush(block)

        P1A_SV = (P.cnt.get("sv0", 0), P.cnt.get("sv1", 0))
        with ExitStack() as p1b:
            P.barrier()
            E1 = p1b.enter_context
            qev = E1(nc.sbuf_tensor("qev", [128, 4, 512], BF16))
            qsq = E1(nc.sbuf_tensor("qsq", [128, C, 512], BF16))
            wgi = E1(nc.sbuf_tensor("wgi", [128, 2, C, 128], I32))
            wui = E1(nc.sbuf_tensor("wui", [128, 2, C, 128], I32))
            wgb = E1(nc.sbuf_tensor("wgb", [128, 2, C, 128], BF16))
            wub = E1(nc.sbuf_tensor("wub", [128, 2, C, 128], BF16))
            sg_bf = E1(nc.sbuf_tensor("sg_bf", [128, 2, 512], BF16))
            h_bf = E1(nc.sbuf_tensor("h_bf", [128, 2, 512], BF16))
            wdi = E1(nc.sbuf_tensor("wdi", [128, 2, D], I32))
            wdb = E1(nc.sbuf_tensor("wdb", [128, 2, D], BF16))

            wq_ve = WQ_VE

            def wd_chunk(j):
                wslot = j % 2
                P.wait("sync", "swd%d" % wslot)
                P.inc("sync", "lwd%d" % wslot, lambda E_, j=j, wslot=wslot:
                      E_.dma_start(out=wdi[:, wslot, :],
                                   in_=wdt[j * 128:(j + 1) * 128, :]), 16)
                P.wait("vector", "lwd%d" % wslot)
                P.wait("vector", "swd%d" % wslot)
                wdv = P.inc("vector", "ve", lambda v, wslot=wslot: v.tensor_scalar_mul(
                    wdb[:, wslot, :], wdi[:, wslot, :], 1.0 / 127.0))
                P.wait("gpsimd", "ve", wdv)
                P.inc("gpsimd", "swd%d" % wslot, lambda E_, j=j, wslot=wslot:
                      E_.dma_start(out=wd_scr[j, :, :], in_=wdb[:, wslot, :]), 16)

            qn_state = {"red_ve": 0}

            def qn_half(half, qsq_sc_at):
                P.wait("tensor", "sc", qsq_sc_at[half * 4 + 3])
                P.wait("tensor", "ve", qn_state["red_ve"])
                for i in range(4):
                    m = half * 4 + i
                    fn = lambda t, m=m, i=i: t.matmul(
                        psB2[32 * i:32 * i + 1, 1536:2048], ones_bf[:], qsq[:, m, :],
                        start=True, stop=True, tile_position=(0, 32 * i))
                    if i == 3:
                        P.inc("tensor", "pe", fn)
                    else:
                        P.op("tensor", fn)
                qn_pe = P.cnt["pe"]
                P.wait("vector", "pe", qn_pe)
                for i in range(4):
                    m = half * 4 + i
                    P.op("vector", lambda v, i=i: v.reduce_max(
                        consts[0:1, 10:11], psB2[32 * i:32 * i + 1, 1536:2048],
                        axis=mybir.AxisListType.X))
                    P.inc("vector", "ve", lambda v, m=m: v.tensor_max(
                        consts[0:1, 16 + m:17 + m], consts[0:1, 16 + m:17 + m],
                        consts[0:1, 10:11]))
                qn_state["red_ve"] = P.cnt["ve"]

            qev_st = [0, 0, 0, 0]
            for nb in range(NBO):
                tb = nb * 512
                qmm_pe_at = {}
                qev_ve_at = {}
                qsq_sc_at = {}
                for m in range(C):
                    jwd = nb * 8 + m
                    if jwd < MH:
                        wd_chunk(jwd)
                    slot = m % 4
                    qbank = BK[1] if m % 2 == 0 else BK[6]
                    P.wait("tensor", "ve", wq_ve)
                    if m >= 2:
                        P.wait("tensor", "ve", qev_ve_at[m - 2])
                    for c in range(C):
                        fn = lambda t, m=m, c=c, tb=tb, qbank=qbank: t.matmul(
                            qbank, wq_bf[:, c, m * 128:(m + 1) * 128],
                            xn_own[:, c, tb:tb + 512], start=(c == 0), stop=(c == C - 1))
                        if c == C - 1:
                            qmm_pe_at[m] = P.inc("tensor", "pe", fn)
                        else:
                            P.op("tensor", fn)
                    P.wait("vector", "pe", qmm_pe_at[m])
                    P.wait("vector", "sq%d" % slot)
                    P.inc("vector", "ve", lambda v, m=m, slot=slot, qbank=qbank:
                          v.tensor_scalar_add(qev[:, slot, :], qbank, bqb[:, m:m + 1]))
                    qev_ve_at[m] = P.cnt["ve"]
                    P.wait("scalar", "ve", qev_ve_at[m])
                    P.inc("scalar", "sc", lambda s, slot=slot, m=m: s.activation(
                        qsq[:, m, :], qev[:, slot, :], AF.Square))
                    qsq_sc_at[m] = P.cnt["sc"]
                    P.wait("gpsimd", "ve", qev_ve_at[m])
                    P.inc("gpsimd", "sq%d" % slot, lambda E_, m=m, tb=tb, slot=slot:
                          E_.dma_start(out=q_scr[m, :, tb:tb + 512],
                                       in_=qev[:, slot, :]), 16)
                    if m == 6:
                        qn_half(0, qsq_sc_at)
                qn_half(1, qsq_sc_at)
            norms_ve = P.cnt["ve"]

            P.wait("scalar", "ve", norms_ve)
            for m in range(HP):
                P.op("scalar", lambda s, m=m: s.activation(
                    consts[0:1, 32:33], consts[0:1, 16 + m:17 + m], AF.Ln,
                    bias=consts[0:1, 1:2]))
                P.inc("scalar", "sc", lambda s, m=m: s.activation(
                    consts[0:1, 33:34], consts[0:1, 24 + m:25 + m], AF.Ln,
                    bias=consts[0:1, 1:2]))
                P.wait("vector", "sc", P.cnt["sc"])
                P.inc("vector", "ve", lambda v: v.tensor_add(
                    consts[0:1, 34:35], consts[0:1, 32:33], consts[0:1, 33:34]))
                P.wait("scalar", "ve", P.cnt["ve"])
                P.inc("scalar", "sc", lambda s: s.activation(
                    consts[0:1, 35:36], consts[0:1, 34:35],
                    AF.Exp, bias=consts[0:1, 2:3], scale=0.5))
                P.wait("vector", "sc", P.cnt["sc"])
                P.inc("vector", "ve", lambda v: v.tensor_scalar_mul(
                    consts[0:1, 36:37], consts[0:1, 35:36], -1.0))
                P.wait("tensor", "ve", P.cnt["ve"])
                P.inc("tensor", "pe", lambda t, m=m: t.matmul(
                    psB2[:, m:m + 1], ones_f1[:], consts[0:1, 36:37],
                    start=True, stop=True))
            P.wait("vector", "pe", P.cnt["pe"])
            P.inc("vector", "ve", lambda v: v.tensor_copy(negc_sb[:], psB2[:, 0:HP]))
            NEGC_VE = P.cnt["ve"]

            h_st = [0, 0]
            wgt3 = wgt.rearrange("(c p) h -> p c h", p=128)
            wut3 = wut.rearrange("(c p) h -> p c h", p=128)
            wld_at = [0, 0]
            deq_ve_at = [0, 0]
            mm_done_at = [0, 0]
            silu_sc_at = {-2: 0, -1: 0}
            hv_at = {-2: 0, -1: 0}
            P.inc("sync", "lg0", lambda E_:
                  E_.dma_start(out=wgi[:, 0], in_=wgt3[:, :, 0:128]), 16)
            P.inc("sync", "lg0", lambda E_:
                  E_.dma_start(out=wui[:, 0], in_=wut3[:, :, 0:128]), 16)
            def deq_for(mm):
                wbuf = mm % 2
                P.wait("vector", "lg%d" % wbuf)
                P.wait("vector", "pe", mm_done_at[wbuf])
                P.op("vector", lambda v, wbuf=wbuf: v.tensor_scalar_mul(
                    wgb[:, wbuf].rearrange("p c f -> p (c f)"),
                    wgi[:, wbuf].rearrange("p c f -> p (c f)"), 1.0 / 127.0))
                P.inc("vector", "ve", lambda v, wbuf=wbuf: v.tensor_scalar_mul(
                    wub[:, wbuf].rearrange("p c f -> p (c f)"),
                    wui[:, wbuf].rearrange("p c f -> p (c f)"), 1.0 / 127.0))
                deq_ve_at[wbuf] = P.cnt["ve"]

            for m in range(MH):
                wbuf = m % 2
                if m + 1 < MH:
                    nbuf = (m + 1) % 2
                    P.wait("sync", "ve", deq_ve_at[nbuf])
                    P.inc("sync", "lg%d" % nbuf, lambda E_, m=m, nbuf=nbuf:
                          E_.dma_start(out=wgi[:, nbuf],
                                       in_=wgt3[:, :, (m + 1) * 128:(m + 2) * 128]), 16)
                    P.inc("sync", "lg%d" % nbuf, lambda E_, m=m, nbuf=nbuf:
                          E_.dma_start(out=wui[:, nbuf],
                                       in_=wut3[:, :, (m + 1) * 128:(m + 2) * 128]), 16)
                if m == 0:
                    deq_for(0)
                for nb in range(NBO):
                    u = m * NBO + nb
                    gb = 2 + 2 * (u % 2)
                    ub = 3 + 2 * (u % 2)
                    tb = nb * 512
                    P.wait("tensor", "ve", deq_ve_at[wbuf])
                    P.wait("tensor", "sc", silu_sc_at[u - 2])
                    for c in range(C):
                        P.op("tensor", lambda t, c=c, tb=tb, wbuf=wbuf, gb=gb: t.matmul(
                            BK[gb], wgb[:, wbuf, c, :], xn_own[:, c, tb:tb + 512],
                            start=(c == 0), stop=(c == C - 1)))
                    P.wait("tensor", "ve", hv_at[u - 2])
                    for c in range(C):
                        fn = lambda t, c=c, tb=tb, wbuf=wbuf, ub=ub: t.matmul(
                            BK[ub], wub[:, wbuf, c, :], xn_own[:, c, tb:tb + 512],
                            start=(c == 0), stop=(c == C - 1))
                        if c == C - 1:
                            P.inc("tensor", "pe", fn)
                        else:
                            P.op("tensor", fn)
                    gu_pe = P.cnt["pe"]
                    mm_done_at[wbuf] = gu_pe
                    P.wait("scalar", "pe", gu_pe)
                    P.wait("scalar", "ve", hv_at[u - 2])
                    P.inc("scalar", "sc", lambda s, u=u, gb=gb: s.activation(
                        sg_bf[:, u % 2, :], BK[gb], AF.Silu))
                    silu_sc_at[u] = P.cnt["sc"]
                    if nb == 2 and m + 1 < MH:
                        deq_for(m + 1)
                    P.wait("vector", "sc", silu_sc_at[u])
                    P.wait("vector", "sh%d" % (u % 2))
                    P.inc("vector", "ve", lambda v, u=u, ub=ub: v.tensor_mul(
                        h_bf[:, u % 2, :], sg_bf[:, u % 2, :], BK[ub]))
                    hv_at[u] = P.cnt["ve"]
                    P.wait("gpsimd", "ve", hv_at[u])
                    P.inc("gpsimd", "sh%d" % (u % 2), lambda E_, m=m, tb=tb, u=u:
                          E_.dma_start(out=h_scr[m, :, tb:tb + 512],
                                       in_=h_bf[:, u % 2, :]), 16)
            P.flush(block)

        ph1wrap.close()

        with ExitStack() as p2:
            P.barrier()
            E2 = p2.enter_context
            v_sb = E2(nc.sbuf_tensor("v_sb", [128, KC, D], BF16))
            k_u = E2(nc.sbuf_tensor("k_u", [128, 2, TA], BF16))
            q_u = E2(nc.sbuf_tensor("q_u", [128, 2, 512], BF16))
            pt_sb = E2(nc.sbuf_tensor("pt_sb", [128, 2048], BF16))
            ctx2 = E2(nc.sbuf_tensor("ctx2", [128, 3, HP, 512], BF16))
            h_qb = E2(nc.sbuf_tensor("h_qb", [128, MH, 512], BF16))
            wd_u = E2(nc.sbuf_tensor("wd_u", [128, 2, MH, 128], BF16))
            ffn_sb = E2(nc.sbuf_tensor("ffn_sb", [128, C, 512], F32))
            x_m = E2(nc.sbuf_tensor("x_m", [128, 2, 512], F32))
            o_m = E2(nc.sbuf_tensor("o_m", [128, 2, 512], F32))
            wo_bf = E2(nc.sbuf_tensor("wo_bf", [128, C, D], BF16))
            rb_sb = E2(nc.sbuf_tensor("rb_sb", [128, 2, 512], F32))
            cstage = E2(nc.sbuf_tensor("cstage", [128, 2, 512], BF16))
            wstage = ffn_sb[:].rearrange("p c t -> p (c t)")[:, 0:1024].rearrange(
                "p (a f) -> p a f", a=1)

            SGA = [psumS[:, 0:1024], psumS[:, 1024:2048]]
            ctxb = psB2[:, 0:512]
            denb = psB2[:, 512:1024]
            FB = [psB2[:, 1024:1536], psB2[:, 1536:2048]]

            h3 = h_scr.rearrange("j p t -> p j t")
            wd3 = wd_scr.rearrange("j p f -> p j f")

            for s_ in ("sk0", "sk1", "sk2", "sk3", "sq0", "sq1", "sq2", "sq3"):
                P.wait("sync", s_)

            kq_ld = [0, 0]
            kq_pe_done = [0, 0]
            units = [(qb, hp) for qb in range(NBO) for hp in range(HP)]
            NU = len(units)

            def load_unit(u):
                qb, hp = units[u]
                slot = u % 2
                P.wait("sync", "pe", kq_pe_done[slot])
                P.inc("sync", "lkq%d" % slot, lambda E_, hp=hp, slot=slot:
                      E_.dma_start(out=k_u[:, slot, :], in_=k_scr[hp, :, :]), 16)
                kq_ld[slot] = P.inc("sync", "lkq%d" % slot,
                                    lambda E_, qb=qb, hp=hp, slot=slot:
                                    E_.dma_start(out=q_u[:, slot, :],
                                                 in_=q_scr[hp, :, qb * 512:(qb + 1) * 512]), 16)
            load_unit(0)

            P.wait("sync", "sv0", P1A_SV[0])
            P.wait("sync", "sv1", P1A_SV[1])
            P.wait("gpsimd", "sv0", P1A_SV[0])
            P.wait("gpsimd", "sv1", P1A_SV[1])
            vs4 = v_scr.rearrange("k p f -> p k f")
            P.inc("sync", "lva", lambda E_: E_.dma_start(
                out=v_sb[:, 0:8, :], in_=vs4[:, 0:8, :]), 16)
            for i, s_ in ((1, "lvb"), (2, "lvc"), (3, "lvd")):
                P.inc("gpsimd", s_, lambda E_, i=i: E_.dma_start(
                    out=v_sb[:, i * 8:(i + 1) * 8, :],
                    in_=vs4[:, i * 8:(i + 1) * 8, :]), 16)
            for s_ in ("sh0", "sh1", "swd0", "swd1"):
                P.wait("sync", s_)

            last_cast = 0
            for gi in range(C):
                P.wait("gpsimd", "ve", last_cast)
                ldv = P.inc("gpsimd", "lw", lambda E_, gi=gi: E_.dma_start(
                    out=wstage[:],
                    in_=wot.rearrange("(c p) f -> p c f", p=128)[:, gi:gi + 1, :]), 16)
                P.wait("vector", "lw")
                last_cast = P.inc("vector", "ve", lambda v, gi=gi:
                                  v.tensor_copy(wo_bf[:, gi:gi + 1, :], wstage[:]))
            wo_ve = last_cast

            state = {"hld": 0, "down_last_pe": 0, "wd_ld": [0, 0],
                     "dn_evac_ve": {}, "x_ld": {}, "fin_ve": {},
                     "o_st": [0, 0], "ctxmul_ve": {}}

            def mk_down_fillers(pq):
                F = []

                def h_load(part, pq=pq):
                    P.wait("sync", "pe", state["down_last_pe"])
                    P.inc("sync", "lh", lambda E_, part=part, pq=pq:
                          E_.dma_start(
                        out=h_qb[:, part * 6:(part + 1) * 6, :],
                        in_=h3[:, part * 6:(part + 1) * 6, pq * 512:(pq + 1) * 512]), 16)
                for part in range(4):
                    F.append((1, lambda part=part: h_load(part)))

                def wd_load(m2):
                    P.wait("sync", "ve", state["dn_evac_ve"].get(
                        m2 - 2, state["dn_evac_ve"].get(m2 + 6, 0)))
                    P.inc("sync", "lwdu%d" % (m2 % 2), lambda E_, m2=m2:
                          E_.dma_start(
                        out=wd_u[:, m2 % 2, :, :],
                        in_=wd3[:, :, m2 * 128:(m2 + 1) * 128]), 16)

                def down_mm(m2, jg):
                    if jg == 0:
                        P.wait("tensor", "lwdu%d" % (m2 % 2))
                        P.wait("tensor", "lh")
                        P.wait("tensor", "ve", state["dn_evac_ve"].get(
                            m2 - 2, state["dn_evac_ve"].get(m2 + 6, 0)))
                        P.wait("tensor", "ve", state["fin_ve"].get(
                            m2 - 2, state["fin_ve"].get(m2 + 6, 0)))
                    for j in range(jg * 6, jg * 6 + 6):
                        fn = lambda t, m2=m2, j=j, jg=jg: t.matmul(
                            FB[m2 % 2], wd_u[:, m2 % 2, j, :], h_qb[:, j, :],
                            start=(j == 0), stop=(j == MH - 1))
                        if j == jg * 6 + 5:
                            P.inc("tensor", "pe", fn)
                        else:
                            P.op("tensor", fn)
                    if jg == 3:
                        state["down_last_pe"] = P.cnt["pe"]
                        P.wait("vector", "pe", P.cnt["pe"])
                        P.inc("vector", "ve", lambda v, m2=m2: v.tensor_copy(
                            ffn_sb[:, m2, :], FB[m2 % 2]))
                        state["dn_evac_ve"][m2] = P.cnt["ve"]
                F.append((1, lambda: wd_load(0)))
                F.append((1, lambda: wd_load(1)))
                for m2 in range(C):
                    for jg in range(4):
                        F.append((2, lambda m2=m2, jg=jg: down_mm(m2, jg)))
                    if m2 + 2 < C:
                        F.append((1, lambda m2=m2: wd_load(m2 + 2)))
                return F

            def mk_out_fillers(pq):
                tq = pq * 512
                F = []

                def x_load(m, pq=pq):
                    P.wait("sync", "ve", state["fin_ve"].get(
                        m - 2, state["fin_ve"].get(m + 6, 0)))
                    P.inc("sync", "lxm%d" % (m % 2), lambda E_, m=m, pq=pq:
                          E_.dma_start(
                        out=x_m[:, m % 2, :],
                        in_=xt3[:, m, pq * 512:(pq + 1) * 512]), 16)

                def out_m(m, pq=pq, tq=tq):
                    P.wait("tensor", "ve", max(wo_ve, state["dn_evac_ve"][C - 1],
                                               state["ctxmul_ve"].get(pq, 0)))
                    P.wait("tensor", "ve", state["fin_ve"].get(
                        m - 2, state["fin_ve"].get(m + 6, 0)))
                    for hp in range(HP):
                        fn = lambda t, m=m, hp=hp, pq=pq: t.matmul(
                            FB[m % 2], wo_bf[:, hp, m * 128:(m + 1) * 128],
                            ctx2[:, pq % 3, hp, :],
                            start=(hp == 0), stop=(hp == HP - 1))
                        if hp == HP - 1:
                            P.inc("tensor", "pe", fn)
                        else:
                            P.op("tensor", fn)
                    op_pe = P.cnt["pe"]
                    P.wait("vector", "pe", op_pe)
                    P.wait("vector", "lxm%d" % (m % 2))
                    P.wait("vector", "so%d" % (m % 2))
                    P.op("vector", lambda v, m=m: v.tensor_scalar_add(
                        o_m[:, m % 2, :], FB[m % 2], bob[:, m:m + 1]))
                    P.op("vector", lambda v, m=m: v.tensor_add(
                        o_m[:, m % 2, :], o_m[:, m % 2, :], x_m[:, m % 2, :]))
                    P.inc("vector", "ve", lambda v, m=m: v.tensor_add(
                        o_m[:, m % 2, :], o_m[:, m % 2, :], ffn_sb[:, m, :]))
                    state["fin_ve"][m] = P.cnt["ve"]
                    P.wait("sync", "ve", state["fin_ve"][m])
                    P.inc("sync", "so%d" % (m % 2),
                          lambda E_, m=m, tq=tq: E_.dma_start(
                        out=yt3[:, m, tq:tq + 512], in_=o_m[:, m % 2, :]), 16)
                for m in range(C):
                    F.append((1, lambda m=m: x_load(m)))
                    F.append((4, lambda m=m: out_m(m)))
                return F

            def s_pair(kc, slot):
                g = SGA[kc % 2]
                P.op("tensor", lambda t, kc=kc, g=g, slot=slot: t.matmul(
                    g[:, 0:512],
                    k_u[0:64, slot, kc * 128:(kc + 1) * 128],
                    q_u[0:64, slot, :],
                    start=True, stop=True, tile_position=(0, 0)))
                return P.inc("tensor", "pe", lambda t, kc=kc, g=g, slot=slot: t.matmul(
                    g[:, 512:1024],
                    k_u[64:128, slot, kc * 128:(kc + 1) * 128],
                    q_u[64:128, slot, :],
                    start=True, stop=True, tile_position=(64, 0)))

            fillers = []
            prev_last_pvden = 0
            prev_pv30 = 0
            prev_evac_ve = 0
            carry_s_pe = None
            for u in range(NU):
                qb, hp = units[u]
                slot = u % 2
                if u + 1 < NU:
                    load_unit(u + 1)
                if hp == 0:
                    if qb >= 1:
                        fillers.extend(mk_out_fillers(qb - 1))
                    fillers.extend(mk_down_fillers(qb))

                P.wait("tensor", "lkq%d" % slot, kq_ld[slot])
                P.wait("scalar", "ve", NEGC_VE)
                exp_sc_at = {}
                den4_at = {}
                if carry_s_pe is None:
                    s_pe_at = {kc: s_pair(kc, slot) for kc in range(2)}
                else:
                    s_pe_at = carry_s_pe
                for kc in range(KC):
                    j = kc % 2
                    if u == 0 and kc in (8, 16, 24):
                        P.wait("tensor", {8: "lvb", 16: "lvc", 24: "lvd"}[kc])
                    P.wait("scalar", "pe", s_pe_at[kc])
                    P.wait("scalar", "pe", den4_at.get(kc - 2, prev_pv30 if kc == 0
                                                       else prev_last_pvden))
                    exp_sc_at[kc] = P.inc("scalar", "sc",
                                          lambda s, j=j, hp=hp: s.activation(
                        pt_sb[:, j * 1024:(j + 1) * 1024],
                        SGA[j],
                        AF.Exp, bias=negc_sb[:, hp:hp + 1], scale=0.125))
                    P.wait("tensor", "sc", exp_sc_at[kc])
                    if kc == 0:
                        if u == 0:
                            P.wait("tensor", "lva")
                        P.wait("tensor", "ve", prev_evac_ve)
                    P.op("tensor", lambda t, kc=kc, j=j, hp=hp: t.matmul(
                        ctxb[0:64, :], v_sb[:, kc, (2 * hp) * 64:(2 * hp + 1) * 64],
                        pt_sb[:, j * 1024:j * 1024 + 512],
                        start=(kc == 0), stop=(kc == KC - 1), tile_position=(0, 0)))
                    P.op("tensor", lambda t, kc=kc, j=j, hp=hp: t.matmul(
                        ctxb[64:128, :], v_sb[:, kc, (2 * hp + 1) * 64:(2 * hp + 2) * 64],
                        pt_sb[:, j * 1024 + 512:(j + 1) * 1024],
                        start=(kc == 0), stop=(kc == KC - 1), tile_position=(0, 64)))
                    P.op("tensor", lambda t, kc=kc, j=j: t.matmul(
                        denb[0:1, :], ones_bf[:],
                        pt_sb[:, j * 1024:j * 1024 + 512],
                        start=(kc == 0), stop=(kc == KC - 1), tile_position=(0, 0)))
                    den4_at[kc] = P.inc("tensor", "pe", lambda t, kc=kc, j=j: t.matmul(
                        denb[32:33, :], ones_bf[:],
                        pt_sb[:, j * 1024 + 512:(j + 1) * 1024],
                        start=(kc == 0), stop=(kc == KC - 1), tile_position=(0, 32)))
                    if kc + 2 < KC:
                        s_pe_at[kc + 2] = s_pair(kc + 2, slot)
                    if kc % 4 == 3 and fillers and fillers[0][0] <= FILLER_MODE:
                        fillers.pop(0)[1]()
                prev_last_pvden = den4_at[KC - 1]
                prev_pv30 = den4_at[KC - 2]
                carry_s_pe = None
                if u + 1 < NU:
                    nslot = (u + 1) % 2
                    P.wait("tensor", "lkq%d" % nslot, kq_ld[nslot])
                    carry_s_pe = {kc: s_pair(kc, nslot) for kc in range(2)}
                for _ in range(2):
                    if fillers and fillers[0][0] <= FILLER_MODE:
                        fillers.pop(0)[1]()
                P.wait("vector", "pe", prev_last_pvden)
                P.op("vector", lambda v: v.tensor_copy(r_var[:], denb[0:1, :]))
                P.inc("vector", "ve", lambda v: v.tensor_copy(r_rstd[:], denb[32:33, :]))
                den_ve = P.cnt["ve"]
                P.wait("scalar", "ve", den_ve)
                P.op("scalar", lambda s: s.activation(
                    r_mean[:], r_var[:], AF.Ln, bias=consts[0:1, 1:2]))
                P.op("scalar", lambda s: s.activation(
                    r_msq[:], r_rstd[:], AF.Ln, bias=consts[0:1, 1:2]))
                P.op("scalar", lambda s: s.activation(
                    r_rA[:], r_mean[:], AF.Exp, bias=consts[0:1, 1:2], scale=-1.0))
                P.inc("scalar", "sc", lambda s: s.activation(
                    r_rB[:], r_msq[:], AF.Exp, bias=consts[0:1, 1:2], scale=-1.0))
                rec_sc = P.cnt["sc"]
                P.wait("tensor", "sc", rec_sc)
                P.op("tensor", lambda t: t.matmul(denb[0:64, :], ones_f1[0:1, 0:64],
                                                  r_rA[:], start=True, stop=True,
                                                  tile_position=(0, 0)))
                P.inc("tensor", "pe", lambda t: t.matmul(denb[64:128, :], ones_f1[0:1, 0:64],
                                                         r_rB[:], start=True, stop=True,
                                                         tile_position=(0, 64)))
                rb_pe = P.cnt["pe"]
                P.wait("vector", "pe", rb_pe)
                P.op("vector", lambda v, slot=slot: v.tensor_copy(
                    rb_sb[:, slot, :], psB2[:, 512:1024]))
                P.inc("vector", "ve", lambda v, qb=qb, hp=hp, slot=slot: v.tensor_mul(
                    ctx2[:, qb % 3, hp, :], ctxb, rb_sb[:, slot, :]))
                state["ctxmul_ve"][qb] = P.cnt["ve"]
                prev_evac_ve = P.cnt["ve"]
                kq_pe_done[slot] = P.cnt["pe"]
            P.wait("tensor", "ve", P.cnt["ve"])
            P.wait("scalar", "ve", P.cnt["ve"])
            if DRAIN_ON:
                for tag, f in fillers:
                    f()
                for tag, f in mk_out_fillers(NBO - 1):
                    f()
            if dump:
                P.wait("gpsimd", "ve", P.cnt["ve"])
                P.inc("gpsimd", "sdbg", lambda E_: E_.dma_start(out=dbg, in_=consts[:]), 16)

            P.dma_done("gpsimd")
            P.flush(block)

    return nc


_NC_CACHE = None


def kernel(L0_attn_x, L0_attn_W_q, L0_attn_b_q, L0_attn_W_k, L0_attn_b_k,
           L0_attn_W_v, L0_attn_b_v, L0_attn_W_out, L0_attn_b_out,
           L0_ffn_w_gate, L0_ffn_w_up, L0_ffn_w_down):
    global _NC_CACHE
    x = np.asarray(L0_attn_x, dtype=np.float32)
    wq = np.asarray(L0_attn_W_q, dtype=np.float32)
    wk = np.asarray(L0_attn_W_k, dtype=np.float32)
    wv = np.asarray(L0_attn_W_v, dtype=np.float32)
    wo = np.asarray(L0_attn_W_out, dtype=np.float32)
    bq = np.asarray(L0_attn_b_q, dtype=np.float32)
    bk = np.asarray(L0_attn_b_k, dtype=np.float32)
    bv = np.asarray(L0_attn_b_v, dtype=np.float32)
    bo = np.asarray(L0_attn_b_out, dtype=np.float32)
    wg = np.asarray(L0_ffn_w_gate, dtype=np.int32)
    wu = np.asarray(L0_ffn_w_up, dtype=np.int32)
    wd = np.asarray(L0_ffn_w_down, dtype=np.int32)

    shared = {
        "wqt": np.ascontiguousarray(wq.T), "wkt": np.ascontiguousarray(wk.T),
        "wvt": np.ascontiguousarray(wv.T), "wot": np.ascontiguousarray(wo.T),
        "bq": bq, "bk": bk, "bv": bv, "bo": bo,
        "wgt": np.ascontiguousarray(wg.T), "wut": np.ascontiguousarray(wu.T),
        "wdt": np.ascontiguousarray(wd.T),
    }
    in_maps = []
    for c in range(8):
        b, hf = c // 2, c % 2
        xb = x[b]
        own = xb[hf * 2048:(hf + 1) * 2048]
        oth = xb[(1 - hf) * 2048:(2 - hf) * 2048]
        xcore = np.concatenate([own, oth], axis=0)
        m = dict(shared)
        m["xt"] = np.ascontiguousarray(xcore.T)
        in_maps.append(m)

    if _NC_CACHE is None:
        _NC_CACHE = build_nc()
    res = run_bass_kernel_spmd(_NC_CACHE, in_maps, core_ids=list(range(8)))
    out = np.empty((4, 4096, 1024), dtype=np.float32)
    for c in range(8):
        b, hf = c // 2, c % 2
        out[b, hf * 2048:(hf + 1) * 2048] = np.asarray(res.results[c]["yt"]).T
    return out


# version 36 (final)
# speedup vs baseline: 1.0405x; 1.0405x over previous
import numpy as np
import ml_dtypes
from contextlib import ExitStack

import concourse.bass as bass
import concourse.mybir as mybir
from concourse.bass_utils import run_bass_kernel_spmd

F32 = mybir.dt.float32
BF16 = mybir.dt.bfloat16
I32 = mybir.dt.int32
BF = ml_dtypes.bfloat16
AF = mybir.ActivationFunctionType
ALU = mybir.AluOpType

FILLER_MODE = 4
DRAIN_ON = True
POPLOG = []
D = 1024
C = 8
TA = 4096
TO = 2048
NBA = 8
NBO = 4
HID = 3072
MH = 24
KC = 32
HP = 8
EPS = 1e-5
NEG_LN8 = -2.0794415416798357


class Prog:

    ENGINES = ("sync", "tensor", "vector", "scalar", "gpsimd")

    def __init__(self, nc, ctx):
        self.nc = nc
        self.ctx = ctx
        self.sems = {}
        self.producer = {}
        self.cnt = {}
        self.lastwait = {}
        self.q = {e: [] for e in self.ENGINES}

    def _sem(self, name):
        if name not in self.sems:
            self.sems[name] = self.ctx.enter_context(self.nc.semaphore(name))
            self.cnt[name] = 0
        return self.sems[name]

    def op(self, eng, fn):
        self.q[eng].append(fn)

    def inc(self, eng, sem, fn, n=1):
        s = self._sem(sem)
        self.producer.setdefault(sem, eng)
        self.cnt[sem] += n
        self.q[eng].append(lambda E, fn=fn, s=s, n=n: fn(E).then_inc(s, n))
        return self.cnt[sem]

    def wait(self, eng, sem, val=None):
        self._sem(sem)
        val = self.cnt[sem] if val is None else val
        assert val <= self.cnt[sem], (eng, sem, val, self.cnt[sem])
        key = (eng, sem)
        if self.lastwait.get(key, 0) >= val:
            return
        self.lastwait[key] = val
        s = self.sems[sem]
        self.q[eng].append(lambda E, s=s, v=val: E.wait_ge(s, v))

    def barrier(self):
        for eng in self.ENGINES:
            for sem in list(self.sems):
                if self.producer.get(sem, None) != eng:
                    self.wait(eng, sem)

    def dma_done(self, eng):
        for sem in list(self.sems):
            if self.producer.get(sem) in ("sync", "gpsimd") and                     self.producer.get(sem) != eng:
                self.wait(eng, sem)

    def flush(self, block):
        for eng in self.ENGINES:
            ops = self.q[eng]
            self.q[eng] = []
            if not ops:
                continue
            getattr(block, eng)(lambda E, ops=ops: [f(E) for f in ops])


def build_nc(dump=False):
    nc = bass.Bass("TRN2", target_bir_lowering=False, detect_race_conditions=False)
    _orig_mm = nc.tensor.matmul

    def _mm(*a, **k):
        k.setdefault("skip_group_check", True)
        return _orig_mm(*a, **k)
    try:
        nc.tensor.matmul = _mm
        mm_patched = True
    except (AttributeError, TypeError):
        mm_patched = False
    scratch_kind = "ExternalOutput" if dump else "Internal"

    xt = nc.dram_tensor("xt", [D, TA], F32, kind="ExternalInput").ap()
    wqt = nc.dram_tensor("wqt", [D, D], F32, kind="ExternalInput").ap()
    wkt = nc.dram_tensor("wkt", [D, D], F32, kind="ExternalInput").ap()
    wvt = nc.dram_tensor("wvt", [D, D], F32, kind="ExternalInput").ap()
    wot = nc.dram_tensor("wot", [D, D], F32, kind="ExternalInput").ap()
    bq = nc.dram_tensor("bq", [D], F32, kind="ExternalInput").ap()
    bk = nc.dram_tensor("bk", [D], F32, kind="ExternalInput").ap()
    bv = nc.dram_tensor("bv", [D], F32, kind="ExternalInput").ap()
    bo = nc.dram_tensor("bo", [D], F32, kind="ExternalInput").ap()
    wgt = nc.dram_tensor("wgt", [D, HID], I32, kind="ExternalInput").ap()
    wut = nc.dram_tensor("wut", [D, HID], I32, kind="ExternalInput").ap()
    wdt = nc.dram_tensor("wdt", [HID, D], I32, kind="ExternalInput").ap()
    yt = nc.dram_tensor("yt", [D, TO], F32, kind="ExternalOutput").ap()

    dbg = nc.dram_tensor("dbg", [1, 48], F32, kind=scratch_kind).ap()

    sdt = mybir.dt.uint16 if dump else BF16

    def _scr(name, shape):
        ap = nc.dram_tensor(name, shape, sdt, kind=scratch_kind).ap()
        return ap.bitcast(BF16) if dump else ap
    k_scr = _scr("k_scr", [HP, 128, TA])
    q_scr = _scr("q_scr", [HP, 128, TO])
    v_scr = _scr("v_scr", [KC, 128, D])
    h_scr = _scr("h_scr", [MH, 128, TO])
    wd_scr = _scr("wd_scr", [MH, 128, D])

    xt3 = xt.rearrange("(c p) t -> p c t", p=128)
    yt3 = yt.rearrange("(m p) t -> p m t", p=128)

    with ExitStack() as g:
        E = g.enter_context
        P = Prog(nc, g)

        psumS = E(nc.psum_tensor("psumS", [128, 2048], F32))
        psB2 = E(nc.psum_tensor("psB2", [128, 2048], F32))
        psB = psB2[:, 0:1024]
        ctx_ps = psB2[:, 1024:1536]
        den_ps = psB2[:, 1536:2048]
        BK = [psumS[:, i * 512:(i + 1) * 512] for i in range(4)] + \
             [psB2[:, i * 512:(i + 1) * 512] for i in range(4)]

        consts = E(nc.sbuf_tensor("consts", [1, 48], F32))
        ones_f1 = E(nc.sbuf_tensor("ones_f1", [1, 128], F32))
        ones_bf = E(nc.sbuf_tensor("ones_bf", [128, 1], BF16))
        ones1_bf = E(nc.sbuf_tensor("ones1_bf", [1, 128], BF16))
        negc_sb = E(nc.sbuf_tensor("negc_sb", [128, HP], F32))
        bqb = E(nc.sbuf_tensor("bqb", [128, C], F32))
        bkb = E(nc.sbuf_tensor("bkb", [128, C], F32))
        bob = E(nc.sbuf_tensor("bob", [128, C], F32))
        r_mean = E(nc.sbuf_tensor("r_mean", [1, 512], F32))
        r_msq = E(nc.sbuf_tensor("r_msq", [1, 512], F32))
        r_var = E(nc.sbuf_tensor("r_var", [1, 512], F32))
        r_rstd = E(nc.sbuf_tensor("r_rstd", [1, 512], F32))
        r_nmr = E(nc.sbuf_tensor("r_nmr", [1, 512], F32))
        r_rA = E(nc.sbuf_tensor("r_rA", [1, 512], F32))
        r_rB = E(nc.sbuf_tensor("r_rB", [1, 512], F32))

        def memsets(v):
            v.memset(ones_f1[:], 1.0)
            v.memset(ones_bf[:], 1.0)
            v.memset(ones1_bf[:], 1.0)
            v.memset(consts[0:1, 0:1], EPS)
            v.memset(consts[0:1, 1:2], 0.0)
            v.memset(consts[0:1, 2:3], NEG_LN8)
            return v.memset(consts[0:1, 16:32], 0.0)
        P.inc("vector", "ve", memsets)
        MEMSET_VE = P.cnt["ve"]

        for name, src, dst in (("bq", bq, bqb), ("bk", bk, bkb), ("bo", bo, bob)):
            P.inc("sync", "lb", lambda E_, s=src, d=dst:
                  E_.dma_start(out=d[:], in_=s.rearrange("(m p) -> p m", p=128),
                               allow_slow_non_contiguous=True), 16)

        block = E(nc.Block())

        ph1wrap = ExitStack()
        xn_own = ph1wrap.enter_context(nc.sbuf_tensor("xn_own", [128, C, TO], BF16))
        wq_bf = ph1wrap.enter_context(nc.sbuf_tensor("wq_bf", [128, C, D], BF16))
        with ExitStack() as p1a:
            E1 = p1a.enter_context
            wk_bf = E1(nc.sbuf_tensor("wk_bf", [128, C, D], BF16))
            wv_bf = E1(nc.sbuf_tensor("wv_bf", [128, C, D], BF16))
            wstage = E1(nc.sbuf_tensor("wstage", [128, 4, D], F32))
            wstageB = E1(nc.sbuf_tensor("wstageB", [128, 2, D], F32))
            x_stage = E1(nc.sbuf_tensor("x_stage", [128, 2, C, 512], F32))
            xb_bf = E1(nc.sbuf_tensor("xb_bf", [128, C, 512], BF16))
            xsq_bf = E1(nc.sbuf_tensor("xsq_bf", [128, C, 512], BF16))
            xn_blk = E1(nc.sbuf_tensor("xn_blk", [128, 2, C, 512], BF16))
            tmp_sb = E1(nc.sbuf_tensor("tmp_sb", [128, 512], F32))
            kev = E1(nc.sbuf_tensor("kev", [128, 4, 512], BF16))
            ksq = E1(nc.sbuf_tensor("ksq", [128, C, 512], BF16))
            vev = E1(nc.sbuf_tensor("vev", [128, 2, 1024], BF16))
            bv_f = E1(nc.sbuf_tensor("bv_f", [1, D], F32))
            bv_bf = E1(nc.sbuf_tensor("bv_bf", [1, D], BF16))

            bvld = P.inc("sync", "lbv", lambda E_:
                         E_.dma_start(out=bv_f[:], in_=bv.rearrange("(a t) -> a t", a=1)), 16)
            P.wait("vector", "lbv")
            P.inc("vector", "ve", lambda v: v.tensor_copy(bv_bf[:], bv_f[:]))

            st = {"vev_evac_ve": 0, "stat_rd_ve": 0, "stats_pe": 0,
                  "wcast": P.cnt["ve"]}
            xuse_ve = [0, 0]
            xnblk_pe = [0, 0]
            blk = {}
            P.inc("sync", "lx0", lambda E_:
                  E_.dma_start(out=x_stage[:, 0], in_=xt3[:, :, 0:512]), 16)

            def wcast_chunk(wsrc, wdst, gi):
                ws = st["wcnt"] = st.get("wcnt", 0)
                slot = ws % 4
                st["wcnt"] = ws + 1
                P.wait("sync", "ve", st.get("wc%d" % slot, 0))
                P.inc("sync", "lw%d" % slot, lambda E_, s=wsrc, gi=gi, slot=slot:
                      E_.dma_start(
                          out=wstage[:, slot:slot + 1, :],
                          in_=s.rearrange("(c p) f -> p c f", p=128)[:, gi:gi + 1, :]), 16)
                P.wait("vector", "lw%d" % slot)
                st["wc%d" % slot] = P.inc(
                    "vector", "ve", lambda v, d=wdst, gi=gi, slot=slot:
                    v.tensor_copy(d[:, gi:gi + 1, :], wstage[:, slot:slot + 1, :]))
                return st["wc%d" % slot]

            def wcast_chunkB(wsrc, wdst, gi):
                ws = st["wcntB"] = st.get("wcntB", 0)
                slot = ws % 2
                st["wcntB"] = ws + 1
                P.wait("gpsimd", "sc", st.get("wcB%d" % slot, 0))
                P.inc("gpsimd", "lwB%d" % slot, lambda E_, s=wsrc, gi=gi, slot=slot:
                      E_.dma_start(
                          out=wstageB[:, slot:slot + 1, :],
                          in_=s.rearrange("(c p) f -> p c f", p=128)[:, gi:gi + 1, :]), 16)
                P.wait("scalar", "lwB%d" % slot)
                st["wcB%d" % slot] = P.inc(
                    "scalar", "sc", lambda s_, d=wdst, gi=gi, slot=slot:
                    s_.activation(d[:, gi:gi + 1, :], wstageB[:, slot:slot + 1, :],
                                  AF.Copy))
                return st["wcB%d" % slot]

            def preA(b):
                xbuf = b % 2
                if b + 1 < NBA:
                    P.wait("sync", "ve", xuse_ve[(b + 1) % 2])
                    P.inc("sync", "lx%d" % ((b + 1) % 2), lambda E_, b=b:
                          E_.dma_start(out=x_stage[:, (b + 1) % 2],
                                       in_=xt3[:, :, (b + 1) * 512:(b + 2) * 512]), 16)
                P.wait("vector", "lb")
                P.wait("scalar", "lx%d" % xbuf)
                P.wait("scalar", "pe", st["stats_pe"])
                P.op("scalar", lambda s, xbuf=xbuf: s.activation(
                    xsq_bf[:].rearrange("p c t -> p (c t)"),
                    x_stage[:, xbuf].rearrange("p c t -> p (c t)"), AF.Square))
                P.inc("scalar", "sc", lambda s, xbuf=xbuf: s.activation(
                    xb_bf[:].rearrange("p c t -> p (c t)"),
                    x_stage[:, xbuf].rearrange("p c t -> p (c t)"), AF.Copy))
                st["xbf_sc"] = P.cnt["sc"]

            def preStats(b):
                P.wait("tensor", "sc", st["xbf_sc"])
                P.wait("tensor", "ve", st["stat_rd_ve"])
                for c in range(C):
                    P.op("tensor", lambda t, c=c: t.matmul(
                        psumS[32:33, 0:512], ones_bf[:], xb_bf[:, c, :],
                        start=(c == 0), stop=(c == C - 1), tile_position=(0, 32)))
                for c in range(C):
                    fn = lambda t, c=c: t.matmul(
                        psumS[64:65, 0:512], ones_bf[:], xsq_bf[:, c, :],
                        start=(c == 0), stop=(c == C - 1), tile_position=(0, 64))
                    if c == C - 1:
                        P.inc("tensor", "pe", fn)
                    else:
                        P.op("tensor", fn)
                st["stats_pe"] = P.cnt["pe"]
                P.wait("vector", "pe", st["stats_pe"])
                P.op("vector", lambda v: v.tensor_scalar_mul(r_mean[:], psumS[32:33, 0:512], 1.0 / D))
                P.op("vector", lambda v: v.tensor_scalar_mul(r_msq[:], psumS[64:65, 0:512], 1.0 / D))
                P.op("vector", lambda v: v.tensor_mul(tmp_sb[0:1, :], r_mean[:], r_mean[:]))
                P.inc("vector", "ve", lambda v: v.tensor_sub(r_var[:], r_msq[:], tmp_sb[0:1, :]))
                var_ve = P.cnt["ve"]
                st["stat_rd_ve"] = var_ve
                P.wait("scalar", "ve", var_ve)
                P.op("scalar", lambda s: s.activation(tmp_sb[0:1, :], r_var[:], AF.Ln,
                                                      bias=consts[0:1, 0:1]))
                P.inc("scalar", "sc", lambda s: s.activation(r_rstd[:], tmp_sb[0:1, :], AF.Exp,
                                                             bias=consts[0:1, 1:2], scale=-0.5))
                rstd_sc = P.cnt["sc"]
                P.wait("vector", "sc", rstd_sc)
                P.inc("vector", "ve", lambda v: v.scalar_tensor_tensor(
                    r_nmr[:], r_mean[:], -1.0, r_rstd[:], ALU.mult, ALU.mult))
                st["nmr_ve"] = P.cnt["ve"]

            def preBC(b):
                P.wait("tensor", "ve", st["nmr_ve"])
                P.op("tensor", lambda t: t.matmul(psB2[:, 0:512], ones_f1[:], r_rstd[:],
                                                  start=True, stop=True))
                P.inc("tensor", "pe", lambda t: t.matmul(psB2[:, 512:1024], ones_f1[:], r_nmr[:],
                                                         start=True, stop=True))
                st["bc_pe"] = P.cnt["pe"]

            def preApplyPiece(b, c0, c1, first, last):
                own = b < NBO
                tb = b * 512
                xbuf = b % 2
                if first:
                    P.wait("vector", "pe", st["bc_pe"])
                    if own:
                        st["xn_cur"] = xn_own[:, :, tb:tb + 512]
                    else:
                        st["xn_cur"] = xn_blk[:, b % 2]
                        P.wait("vector", "pe", xnblk_pe[b % 2])
                xn_cur = st["xn_cur"]
                for c in range(c0, c1):
                    P.op("vector", lambda v, c=c, xbuf=xbuf: v.tensor_mul(
                        tmp_sb[:], x_stage[:, xbuf, c, :], psB2[:, 0:512]))
                    P.op("vector", lambda v, c=c, xn_cur=xn_cur: v.tensor_add(
                        xn_cur[:, c, :], tmp_sb[:], psB2[:, 512:1024]))
                if last:
                    P.inc("vector", "ve", lambda v: v.engine_nop())
                    xuse_ve[xbuf] = P.cnt["ve"]
                    blk[b] = (xn_cur, P.cnt["ve"])

            def preApply(b):
                preApplyPiece(b, 0, C, True, True)

            def pre_K(b):
                preA(b)
                preStats(b)
                preBC(b)
                preApply(b)

            def KV(b, nxt=None):
                own = b < NBO
                tb = b * 512
                xn_cur, xn_ve = blk[b]
                kmm_pe_at = {}
                ksq_sc_at = {}
                kev_ve_at = {}
                if nxt is not None and b == 0:
                    preA(nxt)
                for m in range(C):
                    slot = m % 4
                    kbank = BK[1] if m % 2 == 0 else BK[6]
                    if nxt is not None:
                        if m == 2:
                            preStats(nxt)
                        elif m == 4:
                            preBC(nxt)
                            preApplyPiece(nxt, 0, 3, True, False)
                        elif m == 5:
                            preApplyPiece(nxt, 3, 6, False, False)
                        elif m == 6:
                            preApplyPiece(nxt, 6, C, False, True)
                            if nxt + 1 < NBA:
                                preA(nxt + 1)
                    P.wait("tensor", "ve", xn_ve)
                    if m == 0 and b == 0:
                        P.wait("tensor", "ve", st["wcast"])
                    if m >= 2:
                        P.wait("tensor", "ve", kev_ve_at[m - 2])
                    for c in range(C):
                        fn = lambda t, m=m, c=c, xn_cur=xn_cur, kbank=kbank: t.matmul(
                            kbank, wk_bf[:, c, m * 128:(m + 1) * 128],
                            xn_cur[:, c, :], start=(c == 0), stop=(c == C - 1))
                        if c == C - 1:
                            kmm_pe_at[m] = P.inc("tensor", "pe", fn)
                        else:
                            P.op("tensor", fn)
                    P.wait("vector", "pe", kmm_pe_at[m])
                    P.wait("vector", "sk%d" % slot)
                    P.inc("vector", "ve", lambda v, m=m, slot=slot, kbank=kbank:
                          v.tensor_scalar_add(kev[:, slot, :], kbank, bkb[:, m:m + 1]))
                    kev_ve_at[m] = P.cnt["ve"]
                    P.wait("scalar", "ve", kev_ve_at[m])
                    P.inc("scalar", "sc", lambda s, slot=slot, m=m: s.activation(
                        ksq[:, m, :], kev[:, slot, :], AF.Square))
                    ksq_sc_at[m] = P.cnt["sc"]
                    P.wait("gpsimd", "ve", kev_ve_at[m])
                    P.inc("gpsimd", "sk%d" % slot, lambda E_, m=m, tb=tb, slot=slot:
                          E_.dma_start(out=k_scr[m, :, tb:tb + 512],
                                       in_=kev[:, slot, :]), 16)
                for tt in range(4):
                    slot = tt % 2
                    if b == 0 and tt == 0:
                        P.wait("tensor", "sc", st["wvcast"])
                    P.wait("tensor", "ve", st["vev_evac_ve"])
                    for half in range(2):
                        bank = psumS[:, 1024 + half * 512: 1536 + half * 512]
                        for c in range(C):
                            P.op("tensor", lambda t, c=c, tt=tt, half=half, xn_cur=xn_cur,
                                 bank=bank: t.matmul(
                                     bank, xn_cur[:, c, tt * 128:(tt + 1) * 128],
                                     wv_bf[:, c, half * 512:(half + 1) * 512],
                                     start=(c == 0), stop=False))
                        fn = lambda t, half=half, bank=bank: t.matmul(
                            bank, ones1_bf[:], bv_bf[0:1, half * 512:(half + 1) * 512],
                            start=False, stop=True)
                        if half == 1:
                            P.inc("tensor", "pe", fn)
                        else:
                            P.op("tensor", fn)
                    vmm_pe = P.cnt["pe"]
                    P.wait("vector", "pe", vmm_pe)
                    P.wait("vector", "sv%d" % slot)
                    P.inc("vector", "ve", lambda v, slot=slot: v.tensor_copy(
                        vev[:, slot, :], psumS[:, 1024:2048]))
                    vev_ve = P.cnt["ve"]
                    st["vev_evac_ve"] = vev_ve
                    P.wait("gpsimd", "ve", vev_ve)
                    P.inc("gpsimd", "sv%d" % slot, lambda E_, b=b, tt=tt, slot=slot:
                          E_.dma_start(out=v_scr[b * 4 + tt, :, :],
                                       in_=vev[:, slot, :]), 16)
                if not own:
                    xnblk_pe[b % 2] = P.cnt["pe"]
                for half in range(2):
                    P.wait("tensor", "sc", ksq_sc_at[half * 4 + 3])
                    P.wait("tensor", "ve", st.get("kn_red_ve", 0))
                    for i in range(4):
                        m = half * 4 + i
                        fn = lambda t, m=m, i=i: t.matmul(
                            psB2[32 * i:32 * i + 1, 1536:2048], ones_bf[:], ksq[:, m, :],
                            start=True, stop=True, tile_position=(0, 32 * i))
                        if i == 3:
                            P.inc("tensor", "pe", fn)
                        else:
                            P.op("tensor", fn)
                    kn_pe = P.cnt["pe"]
                    P.wait("vector", "pe", kn_pe)
                    for i in range(4):
                        m = half * 4 + i
                        P.op("vector", lambda v, i=i: v.reduce_max(
                            consts[0:1, 10:11], psB2[32 * i:32 * i + 1, 1536:2048],
                            axis=mybir.AxisListType.X))
                        P.inc("vector", "ve", lambda v, m=m: v.tensor_max(
                            consts[0:1, 24 + m:25 + m], consts[0:1, 24 + m:25 + m],
                            consts[0:1, 10:11]))
                    st["kn_red_ve"] = P.cnt["ve"]

            pre_K(0)
            for gidx in range(C):
                st["wcast"] = wcast_chunk(wkt, wk_bf, gidx)
                st["wvcast"] = wcast_chunkB(wvt, wv_bf, gidx)
            WQ_VE = 0
            for b in range(NBA):
                KV(b, nxt=(b + 1 if b + 1 < NBA else None))
                if b >= 4:
                    for gidx in (2 * (b - 4), 2 * (b - 4) + 1):
                        WQ_VE = wcast_chunk(wqt, wq_bf, gidx)
            P.flush(block)

        P1A_SV = (P.cnt.get("sv0", 0), P.cnt.get("sv1", 0))
        with ExitStack() as p1b:
            P.barrier()
            E1 = p1b.enter_context
            qev = E1(nc.sbuf_tensor("qev", [128, 4, 512], BF16))
            qsq = E1(nc.sbuf_tensor("qsq", [128, C, 512], BF16))
            wgi = E1(nc.sbuf_tensor("wgi", [128, 2, C, 128], I32))
            wui = E1(nc.sbuf_tensor("wui", [128, 2, C, 128], I32))
            wgb = E1(nc.sbuf_tensor("wgb", [128, 2, C, 128], BF16))
            wub = E1(nc.sbuf_tensor("wub", [128, 2, C, 128], BF16))
            sg_bf = E1(nc.sbuf_tensor("sg_bf", [128, 2, 512], BF16))
            h_bf = E1(nc.sbuf_tensor("h_bf", [128, 2, 512], BF16))
            wdi = E1(nc.sbuf_tensor("wdi", [128, 2, D], I32))
            wdb = E1(nc.sbuf_tensor("wdb", [128, 2, D], BF16))

            wq_ve = WQ_VE

            def wd_chunk(j):
                wslot = j % 2
                P.wait("sync", "swd%d" % wslot)
                P.inc("sync", "lwd%d" % wslot, lambda E_, j=j, wslot=wslot:
                      E_.dma_start(out=wdi[:, wslot, :],
                                   in_=wdt[j * 128:(j + 1) * 128, :]), 16)
                P.wait("vector", "lwd%d" % wslot)
                P.wait("vector", "swd%d" % wslot)
                wdv = P.inc("vector", "ve", lambda v, wslot=wslot: v.tensor_scalar_mul(
                    wdb[:, wslot, :], wdi[:, wslot, :], 1.0 / 127.0))
                P.wait("gpsimd", "ve", wdv)
                P.inc("gpsimd", "swd%d" % wslot, lambda E_, j=j, wslot=wslot:
                      E_.dma_start(out=wd_scr[j, :, :], in_=wdb[:, wslot, :]), 16)

            qn_state = {"red_ve": 0}

            def qn_half(half, qsq_sc_at):
                P.wait("tensor", "sc", qsq_sc_at[half * 4 + 3])
                P.wait("tensor", "ve", qn_state["red_ve"])
                for i in range(4):
                    m = half * 4 + i
                    fn = lambda t, m=m, i=i: t.matmul(
                        psB2[32 * i:32 * i + 1, 1536:2048], ones_bf[:], qsq[:, m, :],
                        start=True, stop=True, tile_position=(0, 32 * i))
                    if i == 3:
                        P.inc("tensor", "pe", fn)
                    else:
                        P.op("tensor", fn)
                qn_pe = P.cnt["pe"]
                P.wait("vector", "pe", qn_pe)
                for i in range(4):
                    m = half * 4 + i
                    P.op("vector", lambda v, i=i: v.reduce_max(
                        consts[0:1, 10:11], psB2[32 * i:32 * i + 1, 1536:2048],
                        axis=mybir.AxisListType.X))
                    P.inc("vector", "ve", lambda v, m=m: v.tensor_max(
                        consts[0:1, 16 + m:17 + m], consts[0:1, 16 + m:17 + m],
                        consts[0:1, 10:11]))
                qn_state["red_ve"] = P.cnt["ve"]

            qev_st = [0, 0, 0, 0]
            for nb in range(NBO):
                tb = nb * 512
                qmm_pe_at = {}
                qev_ve_at = {}
                qsq_sc_at = {}
                for m in range(C):
                    jwd = nb * 8 + m
                    if jwd < MH:
                        wd_chunk(jwd)
                    slot = m % 4
                    qbank = BK[1] if m % 2 == 0 else BK[6]
                    P.wait("tensor", "ve", wq_ve)
                    if m >= 2:
                        P.wait("tensor", "ve", qev_ve_at[m - 2])
                    for c in range(C):
                        fn = lambda t, m=m, c=c, tb=tb, qbank=qbank: t.matmul(
                            qbank, wq_bf[:, c, m * 128:(m + 1) * 128],
                            xn_own[:, c, tb:tb + 512], start=(c == 0), stop=(c == C - 1))
                        if c == C - 1:
                            qmm_pe_at[m] = P.inc("tensor", "pe", fn)
                        else:
                            P.op("tensor", fn)
                    P.wait("vector", "pe", qmm_pe_at[m])
                    P.wait("vector", "sq%d" % slot)
                    P.inc("vector", "ve", lambda v, m=m, slot=slot, qbank=qbank:
                          v.tensor_scalar_add(qev[:, slot, :], qbank, bqb[:, m:m + 1]))
                    qev_ve_at[m] = P.cnt["ve"]
                    P.wait("scalar", "ve", qev_ve_at[m])
                    P.inc("scalar", "sc", lambda s, slot=slot, m=m: s.activation(
                        qsq[:, m, :], qev[:, slot, :], AF.Square))
                    qsq_sc_at[m] = P.cnt["sc"]
                    P.wait("gpsimd", "ve", qev_ve_at[m])
                    P.inc("gpsimd", "sq%d" % slot, lambda E_, m=m, tb=tb, slot=slot:
                          E_.dma_start(out=q_scr[m, :, tb:tb + 512],
                                       in_=qev[:, slot, :]), 16)
                    if m == 6:
                        qn_half(0, qsq_sc_at)
                qn_half(1, qsq_sc_at)
            norms_ve = P.cnt["ve"]

            P.wait("scalar", "ve", norms_ve)
            for m in range(HP):
                P.op("scalar", lambda s, m=m: s.activation(
                    consts[0:1, 32:33], consts[0:1, 16 + m:17 + m], AF.Ln,
                    bias=consts[0:1, 1:2]))
                P.inc("scalar", "sc", lambda s, m=m: s.activation(
                    consts[0:1, 33:34], consts[0:1, 24 + m:25 + m], AF.Ln,
                    bias=consts[0:1, 1:2]))
                P.wait("vector", "sc", P.cnt["sc"])
                P.inc("vector", "ve", lambda v: v.tensor_add(
                    consts[0:1, 34:35], consts[0:1, 32:33], consts[0:1, 33:34]))
                P.wait("scalar", "ve", P.cnt["ve"])
                P.inc("scalar", "sc", lambda s: s.activation(
                    consts[0:1, 35:36], consts[0:1, 34:35],
                    AF.Exp, bias=consts[0:1, 2:3], scale=0.5))
                P.wait("vector", "sc", P.cnt["sc"])
                P.inc("vector", "ve", lambda v: v.tensor_scalar_mul(
                    consts[0:1, 36:37], consts[0:1, 35:36], -1.0))
                P.wait("tensor", "ve", P.cnt["ve"])
                P.inc("tensor", "pe", lambda t, m=m: t.matmul(
                    psB2[:, m:m + 1], ones_f1[:], consts[0:1, 36:37],
                    start=True, stop=True))
            P.wait("vector", "pe", P.cnt["pe"])
            P.inc("vector", "ve", lambda v: v.tensor_copy(negc_sb[:], psB2[:, 0:HP]))
            NEGC_VE = P.cnt["ve"]

            h_st = [0, 0]
            wgt3 = wgt.rearrange("(c p) h -> p c h", p=128)
            wut3 = wut.rearrange("(c p) h -> p c h", p=128)
            wld_at = [0, 0]
            deq_ve_at = [0, 0]
            mm_done_at = [0, 0]
            silu_sc_at = {-2: 0, -1: 0}
            hv_at = {-2: 0, -1: 0}
            P.inc("sync", "lg0", lambda E_:
                  E_.dma_start(out=wgi[:, 0], in_=wgt3[:, :, 0:128]), 16)
            P.inc("sync", "lg0", lambda E_:
                  E_.dma_start(out=wui[:, 0], in_=wut3[:, :, 0:128]), 16)
            def deq_for(mm):
                wbuf = mm % 2
                P.wait("vector", "lg%d" % wbuf)
                P.wait("vector", "pe", mm_done_at[wbuf])
                P.op("vector", lambda v, wbuf=wbuf: v.tensor_scalar_mul(
                    wgb[:, wbuf].rearrange("p c f -> p (c f)"),
                    wgi[:, wbuf].rearrange("p c f -> p (c f)"), 1.0 / 127.0))
                P.inc("vector", "ve", lambda v, wbuf=wbuf: v.tensor_scalar_mul(
                    wub[:, wbuf].rearrange("p c f -> p (c f)"),
                    wui[:, wbuf].rearrange("p c f -> p (c f)"), 1.0 / 127.0))
                deq_ve_at[wbuf] = P.cnt["ve"]

            for m in range(MH):
                wbuf = m % 2
                if m + 1 < MH:
                    nbuf = (m + 1) % 2
                    P.wait("sync", "ve", deq_ve_at[nbuf])
                    P.inc("sync", "lg%d" % nbuf, lambda E_, m=m, nbuf=nbuf:
                          E_.dma_start(out=wgi[:, nbuf],
                                       in_=wgt3[:, :, (m + 1) * 128:(m + 2) * 128]), 16)
                    P.inc("sync", "lg%d" % nbuf, lambda E_, m=m, nbuf=nbuf:
                          E_.dma_start(out=wui[:, nbuf],
                                       in_=wut3[:, :, (m + 1) * 128:(m + 2) * 128]), 16)
                if m == 0:
                    deq_for(0)
                for nb in range(NBO):
                    u = m * NBO + nb
                    gb = 2 + 2 * (u % 2)
                    ub = 3 + 2 * (u % 2)
                    tb = nb * 512
                    P.wait("tensor", "ve", deq_ve_at[wbuf])
                    P.wait("tensor", "sc", silu_sc_at[u - 2])
                    for c in range(C):
                        P.op("tensor", lambda t, c=c, tb=tb, wbuf=wbuf, gb=gb: t.matmul(
                            BK[gb], wgb[:, wbuf, c, :], xn_own[:, c, tb:tb + 512],
                            start=(c == 0), stop=(c == C - 1)))
                    P.wait("tensor", "ve", hv_at[u - 2])
                    for c in range(C):
                        fn = lambda t, c=c, tb=tb, wbuf=wbuf, ub=ub: t.matmul(
                            BK[ub], wub[:, wbuf, c, :], xn_own[:, c, tb:tb + 512],
                            start=(c == 0), stop=(c == C - 1))
                        if c == C - 1:
                            P.inc("tensor", "pe", fn)
                        else:
                            P.op("tensor", fn)
                    gu_pe = P.cnt["pe"]
                    mm_done_at[wbuf] = gu_pe
                    P.wait("scalar", "pe", gu_pe)
                    P.wait("scalar", "ve", hv_at[u - 2])
                    P.inc("scalar", "sc", lambda s, u=u, gb=gb: s.activation(
                        sg_bf[:, u % 2, :], BK[gb], AF.Silu))
                    silu_sc_at[u] = P.cnt["sc"]
                    if nb == 2 and m + 1 < MH:
                        deq_for(m + 1)
                    P.wait("vector", "sc", silu_sc_at[u])
                    P.wait("vector", "sh%d" % (u % 2))
                    P.inc("vector", "ve", lambda v, u=u, ub=ub: v.tensor_mul(
                        h_bf[:, u % 2, :], sg_bf[:, u % 2, :], BK[ub]))
                    hv_at[u] = P.cnt["ve"]
                    P.wait("gpsimd", "ve", hv_at[u])
                    P.inc("gpsimd", "sh%d" % (u % 2), lambda E_, m=m, tb=tb, u=u:
                          E_.dma_start(out=h_scr[m, :, tb:tb + 512],
                                       in_=h_bf[:, u % 2, :]), 16)
            P.flush(block)

        ph1wrap.close()

        with ExitStack() as p2:
            P.barrier()
            E2 = p2.enter_context
            v_sb = E2(nc.sbuf_tensor("v_sb", [128, KC, D], BF16))
            k_u = E2(nc.sbuf_tensor("k_u", [128, 2, TA], BF16))
            q_u = E2(nc.sbuf_tensor("q_u", [128, 2, 512], BF16))
            pt_sb = E2(nc.sbuf_tensor("pt_sb", [128, 2048], BF16))
            ctx2 = E2(nc.sbuf_tensor("ctx2", [128, 3, HP, 512], BF16))
            h_qb = E2(nc.sbuf_tensor("h_qb", [128, MH, 512], BF16))
            wd_u = E2(nc.sbuf_tensor("wd_u", [128, 2, MH, 128], BF16))
            ffn_sb = E2(nc.sbuf_tensor("ffn_sb", [128, C, 512], F32))
            x_m = E2(nc.sbuf_tensor("x_m", [128, 2, 512], F32))
            o_m = E2(nc.sbuf_tensor("o_m", [128, 2, 512], F32))
            wo_bf = E2(nc.sbuf_tensor("wo_bf", [128, C, D], BF16))
            rb_sb = E2(nc.sbuf_tensor("rb_sb", [128, 2, 512], F32))
            cstage = E2(nc.sbuf_tensor("cstage", [128, 2, 512], BF16))
            wstage = ffn_sb[:].rearrange("p c t -> p (c t)")[:, 0:1024].rearrange(
                "p (a f) -> p a f", a=1)

            SGA = [psumS[:, 0:1024], psumS[:, 1024:2048]]
            ctxb = psB2[:, 0:512]
            denb = psB2[:, 512:1024]
            FB = [psB2[:, 1024:1536], psB2[:, 1536:2048]]

            h3 = h_scr.rearrange("j p t -> p j t")
            wd3 = wd_scr.rearrange("j p f -> p j f")

            for s_ in ("sk0", "sk1", "sk2", "sk3", "sq0", "sq1", "sq2", "sq3"):
                P.wait("sync", s_)

            kq_ld = [0, 0]
            kq_pe_done = [0, 0]
            units = [(qb, hp) for qb in range(NBO) for hp in range(HP)]
            NU = len(units)

            def load_unit(u):
                qb, hp = units[u]
                slot = u % 2
                P.wait("sync", "pe", kq_pe_done[slot])
                P.inc("sync", "lkq%d" % slot, lambda E_, hp=hp, slot=slot:
                      E_.dma_start(out=k_u[:, slot, :], in_=k_scr[hp, :, :]), 16)
                kq_ld[slot] = P.inc("sync", "lkq%d" % slot,
                                    lambda E_, qb=qb, hp=hp, slot=slot:
                                    E_.dma_start(out=q_u[:, slot, :],
                                                 in_=q_scr[hp, :, qb * 512:(qb + 1) * 512]), 16)
            load_unit(0)

            P.wait("sync", "sv0", P1A_SV[0])
            P.wait("sync", "sv1", P1A_SV[1])
            P.wait("gpsimd", "sv0", P1A_SV[0])
            P.wait("gpsimd", "sv1", P1A_SV[1])
            vs4 = v_scr.rearrange("k p f -> p k f")
            P.inc("sync", "lva", lambda E_: E_.dma_start(
                out=v_sb[:, 0:8, :], in_=vs4[:, 0:8, :]), 16)
            for i, s_ in ((1, "lvb"), (2, "lvc"), (3, "lvd")):
                P.inc("gpsimd", s_, lambda E_, i=i: E_.dma_start(
                    out=v_sb[:, i * 8:(i + 1) * 8, :],
                    in_=vs4[:, i * 8:(i + 1) * 8, :]), 16)
            for s_ in ("sh0", "sh1", "swd0", "swd1"):
                P.wait("sync", s_)

            last_cast = 0
            for gi in range(C):
                P.wait("gpsimd", "ve", last_cast)
                ldv = P.inc("gpsimd", "lw", lambda E_, gi=gi: E_.dma_start(
                    out=wstage[:],
                    in_=wot.rearrange("(c p) f -> p c f", p=128)[:, gi:gi + 1, :]), 16)
                P.wait("vector", "lw")
                last_cast = P.inc("vector", "ve", lambda v, gi=gi:
                                  v.tensor_copy(wo_bf[:, gi:gi + 1, :], wstage[:]))
            wo_ve = last_cast

            state = {"hld": 0, "down_last_pe": 0, "wd_ld": [0, 0],
                     "dn_evac_ve": {}, "x_ld": {}, "fin_ve": {},
                     "o_st": [0, 0], "ctxmul_ve": {}}

            def mk_down_fillers(pq):
                F = []

                def h_load(part, pq=pq):
                    P.wait("sync", "pe", state["down_last_pe"])
                    P.inc("sync", "lh", lambda E_, part=part, pq=pq:
                          E_.dma_start(
                        out=h_qb[:, part * 6:(part + 1) * 6, :],
                        in_=h3[:, part * 6:(part + 1) * 6, pq * 512:(pq + 1) * 512]), 16)
                for part in range(4):
                    F.append((1, lambda part=part: h_load(part)))

                def wd_load(m2):
                    P.wait("sync", "ve", state["dn_evac_ve"].get(
                        m2 - 2, state["dn_evac_ve"].get(m2 + 6, 0)))
                    P.inc("sync", "lwdu%d" % (m2 % 2), lambda E_, m2=m2:
                          E_.dma_start(
                        out=wd_u[:, m2 % 2, :, :],
                        in_=wd3[:, :, m2 * 128:(m2 + 1) * 128]), 16)

                def down_mm(m2, jg):
                    if jg == 0:
                        P.wait("tensor", "lwdu%d" % (m2 % 2))
                        P.wait("tensor", "lh")
                        P.wait("tensor", "ve", state["dn_evac_ve"].get(
                            m2 - 2, state["dn_evac_ve"].get(m2 + 6, 0)))
                        P.wait("tensor", "ve", state["fin_ve"].get(
                            m2 - 2, state["fin_ve"].get(m2 + 6, 0)))
                    for j in range(jg * 6, jg * 6 + 6):
                        fn = lambda t, m2=m2, j=j, jg=jg: t.matmul(
                            FB[m2 % 2], wd_u[:, m2 % 2, j, :], h_qb[:, j, :],
                            start=(j == 0), stop=(j == MH - 1))
                        if j == jg * 6 + 5:
                            P.inc("tensor", "pe", fn)
                        else:
                            P.op("tensor", fn)
                    if jg == 3:
                        state["down_last_pe"] = P.cnt["pe"]
                        P.wait("vector", "pe", P.cnt["pe"])
                        P.inc("vector", "ve", lambda v, m2=m2: v.tensor_copy(
                            ffn_sb[:, m2, :], FB[m2 % 2]))
                        state["dn_evac_ve"][m2] = P.cnt["ve"]
                F.append((1, lambda: wd_load(0)))
                F.append((1, lambda: wd_load(1)))
                for m2 in range(C):
                    for jg in range(4):
                        F.append((2, lambda m2=m2, jg=jg: down_mm(m2, jg)))
                    if m2 + 2 < C:
                        F.append((1, lambda m2=m2: wd_load(m2 + 2)))
                return F

            def mk_out_fillers(pq):
                tq = pq * 512
                F = []

                def x_load(m, pq=pq):
                    P.wait("sync", "ve", state["fin_ve"].get(
                        m - 2, state["fin_ve"].get(m + 6, 0)))
                    P.inc("sync", "lxm%d" % (m % 2), lambda E_, m=m, pq=pq:
                          E_.dma_start(
                        out=x_m[:, m % 2, :],
                        in_=xt3[:, m, pq * 512:(pq + 1) * 512]), 16)

                def out_m(m, pq=pq, tq=tq):
                    P.wait("tensor", "ve", max(wo_ve, state["dn_evac_ve"][C - 1],
                                               state["ctxmul_ve"].get(pq, 0)))
                    P.wait("tensor", "ve", state["fin_ve"].get(
                        m - 2, state["fin_ve"].get(m + 6, 0)))
                    for hp in range(HP):
                        fn = lambda t, m=m, hp=hp, pq=pq: t.matmul(
                            FB[m % 2], wo_bf[:, hp, m * 128:(m + 1) * 128],
                            ctx2[:, pq % 3, hp, :],
                            start=(hp == 0), stop=(hp == HP - 1))
                        if hp == HP - 1:
                            P.inc("tensor", "pe", fn)
                        else:
                            P.op("tensor", fn)
                    op_pe = P.cnt["pe"]
                    P.wait("vector", "pe", op_pe)
                    P.wait("vector", "lxm%d" % (m % 2))
                    P.wait("vector", "so%d" % (m % 2))
                    P.op("vector", lambda v, m=m: v.tensor_scalar_add(
                        o_m[:, m % 2, :], FB[m % 2], bob[:, m:m + 1]))
                    P.op("vector", lambda v, m=m: v.tensor_add(
                        o_m[:, m % 2, :], o_m[:, m % 2, :], x_m[:, m % 2, :]))
                    P.inc("vector", "ve", lambda v, m=m: v.tensor_add(
                        o_m[:, m % 2, :], o_m[:, m % 2, :], ffn_sb[:, m, :]))
                    state["fin_ve"][m] = P.cnt["ve"]
                    P.wait("sync", "ve", state["fin_ve"][m])
                    P.inc("sync", "so%d" % (m % 2),
                          lambda E_, m=m, tq=tq: E_.dma_start(
                        out=yt3[:, m, tq:tq + 512], in_=o_m[:, m % 2, :]), 16)
                for m in range(C):
                    F.append((1, lambda m=m: x_load(m)))
                    F.append((4, lambda m=m: out_m(m)))
                return F

            def s_pair(kc, slot):
                g = SGA[kc % 2]
                P.op("tensor", lambda t, kc=kc, g=g, slot=slot: t.matmul(
                    g[:, 0:512],
                    k_u[0:64, slot, kc * 128:(kc + 1) * 128],
                    q_u[0:64, slot, :],
                    start=True, stop=True, tile_position=(0, 0)))
                return P.inc("tensor", "pe", lambda t, kc=kc, g=g, slot=slot: t.matmul(
                    g[:, 512:1024],
                    k_u[64:128, slot, kc * 128:(kc + 1) * 128],
                    q_u[64:128, slot, :],
                    start=True, stop=True, tile_position=(64, 0)))

            fillers = []
            prev_last_pvden = 0
            prev_pv30 = 0
            prev_evac_ve = 0
            carry_s_pe = None
            for u in range(NU):
                qb, hp = units[u]
                slot = u % 2
                if u + 1 < NU:
                    load_unit(u + 1)
                if hp == 0:
                    if qb >= 1:
                        fillers.extend(mk_out_fillers(qb - 1))
                    fillers.extend(mk_down_fillers(qb))

                P.wait("tensor", "lkq%d" % slot, kq_ld[slot])
                P.wait("scalar", "ve", NEGC_VE)
                exp_sc_at = {}
                den4_at = {}
                if carry_s_pe is None:
                    s_pe_at = {kc: s_pair(kc, slot) for kc in range(2)}
                else:
                    s_pe_at = carry_s_pe
                for kc in range(KC):
                    j = kc % 2
                    if u == 0 and kc in (8, 16, 24):
                        P.wait("tensor", {8: "lvb", 16: "lvc", 24: "lvd"}[kc])
                    P.wait("scalar", "pe", s_pe_at[kc])
                    P.wait("scalar", "pe", den4_at.get(kc - 2, prev_pv30 if kc == 0
                                                       else prev_last_pvden))
                    exp_sc_at[kc] = P.inc("scalar", "sc",
                                          lambda s, j=j, hp=hp: s.activation(
                        pt_sb[:, j * 1024:(j + 1) * 1024],
                        SGA[j],
                        AF.Exp, bias=negc_sb[:, hp:hp + 1], scale=0.125))
                    P.wait("tensor", "sc", exp_sc_at[kc])
                    if kc == 0:
                        if u == 0:
                            P.wait("tensor", "lva")
                        P.wait("tensor", "ve", prev_evac_ve)
                    P.op("tensor", lambda t, kc=kc, j=j, hp=hp: t.matmul(
                        ctxb[0:64, :], v_sb[:, kc, (2 * hp) * 64:(2 * hp + 1) * 64],
                        pt_sb[:, j * 1024:j * 1024 + 512],
                        start=(kc == 0), stop=(kc == KC - 1), tile_position=(0, 0)))
                    P.op("tensor", lambda t, kc=kc, j=j, hp=hp: t.matmul(
                        ctxb[64:128, :], v_sb[:, kc, (2 * hp + 1) * 64:(2 * hp + 2) * 64],
                        pt_sb[:, j * 1024 + 512:(j + 1) * 1024],
                        start=(kc == 0), stop=(kc == KC - 1), tile_position=(0, 64)))
                    P.op("tensor", lambda t, kc=kc, j=j: t.matmul(
                        denb[0:1, :], ones_bf[:],
                        pt_sb[:, j * 1024:j * 1024 + 512],
                        start=(kc == 0), stop=(kc == KC - 1), tile_position=(0, 0)))
                    den4_at[kc] = P.inc("tensor", "pe", lambda t, kc=kc, j=j: t.matmul(
                        denb[32:33, :], ones_bf[:],
                        pt_sb[:, j * 1024 + 512:(j + 1) * 1024],
                        start=(kc == 0), stop=(kc == KC - 1), tile_position=(0, 32)))
                    if kc + 2 < KC:
                        s_pe_at[kc + 2] = s_pair(kc + 2, slot)
                    if kc % 4 == 3 and fillers and fillers[0][0] <= FILLER_MODE:
                        fillers.pop(0)[1]()
                prev_last_pvden = den4_at[KC - 1]
                prev_pv30 = den4_at[KC - 2]
                carry_s_pe = None
                if u + 1 < NU:
                    nslot = (u + 1) % 2
                    P.wait("tensor", "lkq%d" % nslot, kq_ld[nslot])
                    carry_s_pe = {kc: s_pair(kc, nslot) for kc in range(2)}
                P.wait("vector", "pe", prev_last_pvden)
                P.op("vector", lambda v: v.tensor_copy(r_var[:], denb[0:1, :]))
                P.inc("vector", "ve", lambda v: v.tensor_copy(r_rstd[:], denb[32:33, :]))
                den_ve = P.cnt["ve"]
                P.wait("scalar", "ve", den_ve)
                P.op("scalar", lambda s: s.activation(
                    r_mean[:], r_var[:], AF.Ln, bias=consts[0:1, 1:2]))
                P.op("scalar", lambda s: s.activation(
                    r_msq[:], r_rstd[:], AF.Ln, bias=consts[0:1, 1:2]))
                P.op("scalar", lambda s: s.activation(
                    r_rA[:], r_mean[:], AF.Exp, bias=consts[0:1, 1:2], scale=-1.0))
                P.inc("scalar", "sc", lambda s: s.activation(
                    r_rB[:], r_msq[:], AF.Exp, bias=consts[0:1, 1:2], scale=-1.0))
                rec_sc = P.cnt["sc"]
                for _ in range(2):
                    if fillers and fillers[0][0] <= FILLER_MODE:
                        fillers.pop(0)[1]()
                P.wait("tensor", "sc", rec_sc)
                P.op("tensor", lambda t: t.matmul(denb[0:64, :], ones_f1[0:1, 0:64],
                                                  r_rA[:], start=True, stop=True,
                                                  tile_position=(0, 0)))
                P.inc("tensor", "pe", lambda t: t.matmul(denb[64:128, :], ones_f1[0:1, 0:64],
                                                         r_rB[:], start=True, stop=True,
                                                         tile_position=(0, 64)))
                rb_pe = P.cnt["pe"]
                P.wait("vector", "pe", rb_pe)
                P.op("vector", lambda v, slot=slot: v.tensor_copy(
                    rb_sb[:, slot, :], psB2[:, 512:1024]))
                P.inc("vector", "ve", lambda v, qb=qb, hp=hp, slot=slot: v.tensor_mul(
                    ctx2[:, qb % 3, hp, :], ctxb, rb_sb[:, slot, :]))
                state["ctxmul_ve"][qb] = P.cnt["ve"]
                prev_evac_ve = P.cnt["ve"]
                kq_pe_done[slot] = P.cnt["pe"]
            P.wait("tensor", "ve", P.cnt["ve"])
            P.wait("scalar", "ve", P.cnt["ve"])
            if DRAIN_ON:
                for tag, f in fillers:
                    f()
                for tag, f in mk_out_fillers(NBO - 1):
                    f()
            if dump:
                P.wait("gpsimd", "ve", P.cnt["ve"])
                P.inc("gpsimd", "sdbg", lambda E_: E_.dma_start(out=dbg, in_=consts[:]), 16)

            P.dma_done("gpsimd")
            P.flush(block)

    return nc


_NC_CACHE = None


def kernel(L0_attn_x, L0_attn_W_q, L0_attn_b_q, L0_attn_W_k, L0_attn_b_k,
           L0_attn_W_v, L0_attn_b_v, L0_attn_W_out, L0_attn_b_out,
           L0_ffn_w_gate, L0_ffn_w_up, L0_ffn_w_down):
    global _NC_CACHE
    x = np.asarray(L0_attn_x, dtype=np.float32)
    wq = np.asarray(L0_attn_W_q, dtype=np.float32)
    wk = np.asarray(L0_attn_W_k, dtype=np.float32)
    wv = np.asarray(L0_attn_W_v, dtype=np.float32)
    wo = np.asarray(L0_attn_W_out, dtype=np.float32)
    bq = np.asarray(L0_attn_b_q, dtype=np.float32)
    bk = np.asarray(L0_attn_b_k, dtype=np.float32)
    bv = np.asarray(L0_attn_b_v, dtype=np.float32)
    bo = np.asarray(L0_attn_b_out, dtype=np.float32)
    wg = np.asarray(L0_ffn_w_gate, dtype=np.int32)
    wu = np.asarray(L0_ffn_w_up, dtype=np.int32)
    wd = np.asarray(L0_ffn_w_down, dtype=np.int32)

    shared = {
        "wqt": np.ascontiguousarray(wq.T), "wkt": np.ascontiguousarray(wk.T),
        "wvt": np.ascontiguousarray(wv.T), "wot": np.ascontiguousarray(wo.T),
        "bq": bq, "bk": bk, "bv": bv, "bo": bo,
        "wgt": np.ascontiguousarray(wg.T), "wut": np.ascontiguousarray(wu.T),
        "wdt": np.ascontiguousarray(wd.T),
    }
    in_maps = []
    for c in range(8):
        b, hf = c // 2, c % 2
        xb = x[b]
        own = xb[hf * 2048:(hf + 1) * 2048]
        oth = xb[(1 - hf) * 2048:(2 - hf) * 2048]
        xcore = np.concatenate([own, oth], axis=0)
        m = dict(shared)
        m["xt"] = np.ascontiguousarray(xcore.T)
        in_maps.append(m)

    if _NC_CACHE is None:
        _NC_CACHE = build_nc()
    res = run_bass_kernel_spmd(_NC_CACHE, in_maps, core_ids=list(range(8)))
    out = np.empty((4, 4096, 1024), dtype=np.float32)
    for c in range(8):
        b, hf = c // 2, c % 2
        out[b, hf * 2048:(hf + 1) * 2048] = np.asarray(res.results[c]["yt"]).T
    return out
